# version 42
# baseline (speedup 1.0000x reference)
import contextlib
import numpy as np
import ml_dtypes
import concourse.bass as bass
import concourse.mybir as mybir
from concourse.bass_utils import run_bass_kernel_spmd

F32 = mybir.dt.float32
BF16 = mybir.dt.bfloat16
AF = mybir.ActivationFunctionType
ALU = mybir.AluOpType

NCORE = 8
B, S, D = 2, 8192, 1024
DH = 4096
NG = 16
GSZ = 512
TOK = 2048

SBUF_BASE = 16512
OFF_A = SBUF_BASE + 0
OFF_B = SBUF_BASE + 65536
OFF_C = SBUF_BASE + 98304
OFF_D = SBUF_BASE + 172032
C_DEN = OFF_C + 0
C_XG = OFF_C + 8192
C_SQ = OFF_C + 16384
C_TL = OFF_C + 18432
C_KG = OFF_C + 20480
C_VST = OFF_C + 28672
C_P = OFF_C + 45056
C_XLN = OFF_C + 49152
C_RSEL = OFF_C + 57344
C_WK = OFF_C + 61440
C_FREE = OFF_C + 65536
D_ROWS = OFF_D + 0
D_CPOOL = OFF_D + 4096
D_BIAS = OFF_D + 4160
D_SCAL = OFF_D + 4288
D_ONESCOL = OFF_D + 4800
D_ONESROW = OFF_D + 4832
D_ONESROW_BF = OFF_D + 5344
D_BVROW = OFF_D + 5600
D_L = OFF_D + 7648
D_WST = OFF_D + 7904
D_WV = OFF_D + 24288
D_END = OFF_D + 40672
assert D_END <= 229344


def _bf(a):
    return np.ascontiguousarray(a).astype(ml_dtypes.bfloat16)


def build_host_inputs(inputs):
    x = np.asarray(inputs["L0_attn_states"], np.float32)
    def pcol(w):
        w = np.asarray(w, np.float32)
        return np.ascontiguousarray(w.reshape(8, 128, w.shape[1])
                                    .transpose(1, 0, 2))

    def tile4(wt, nco):
        wt = np.asarray(wt, np.float32)
        di = wt.shape[0] // 128
        return np.ascontiguousarray(
            wt.reshape(di, 128, nco, 128).transpose(2, 1, 0, 3))

    w1tl = tile4(np.asarray(inputs["L0_ffn_w1_w"], np.float32).T, 32)
    w2tl = tile4(np.asarray(inputs["L0_ffn_w2_w"], np.float32).T, 32)
    shared = {
        "wq": pcol(inputs["L0_attn_Wq"]),
        "wk": tile4(inputs["L0_attn_Wk"], 8),
        "wv": pcol(inputs["L0_attn_Wv"]),
        "wo": pcol(inputs["L0_attn_Wo"]),
        "w12t": np.ascontiguousarray(np.concatenate([w1tl, w2tl], axis=2)),
        "h1t": tile4(np.asarray(inputs["L0_ffn_h1_w"], np.float32).T, 32),
        "h2t": tile4(np.asarray(inputs["L0_ffn_h2_w"], np.float32).T, 32),
        "w3t": tile4(np.asarray(inputs["L0_ffn_w3_w"], np.float32).T, 8),
    }

    def vec8(v):
        return np.ascontiguousarray(np.asarray(v, np.float32).reshape(8, 128).T)

    def vec32(v):
        return np.ascontiguousarray(np.asarray(v, np.float32).reshape(32, 128).T)

    shared["bq_sb"] = vec8(inputs["L0_attn_bq"])
    shared["bk_sb"] = vec8(inputs["L0_attn_bk"])
    shared["bo_sb"] = vec8(inputs["L0_attn_bo"])
    shared["s3_sb"] = vec8(inputs["L0_ffn_w3_s"])
    shared["s1_sb"] = vec32(inputs["L0_ffn_w1_s"])
    shared["s2_sb"] = vec32(inputs["L0_ffn_w2_s"])
    shared["sh1_sb"] = vec32(inputs["L0_ffn_h1_s"])
    shared["sh2_sb"] = vec32(inputs["L0_ffn_h2_s"])
    shared["bv_row"] = np.ascontiguousarray(
        np.asarray(inputs["L0_attn_bv"], np.float32).reshape(1, D))
    kk = np.arange(128)
    shared["Lmat"] = _bf(np.where(kk[None, :] >= kk[:, None],
                                  np.float32(-1600.0), np.float32(0.0)))
    SEL = np.zeros((8, 128, 128), np.float32)
    for c in range(8):
        for half in range(2):
            h = 2 * c + half
            denrow = h if (h % 2 == 1) else 64 + h
            SEL[c, denrow, half * 64:(half + 1) * 64] = 1.0
    shared["SEL"] = np.ascontiguousarray(SEL.transpose(1, 0, 2))

    xT = [np.ascontiguousarray(x[b].T) for b in range(B)]
    xtl = [np.ascontiguousarray(t.reshape(8, 128, 32, 256)
                                .transpose(2, 1, 0, 3)) for t in xT]
    per_core = []
    for c in range(NCORE):
        b, i = c // 4, c % 4
        x_own = np.ascontiguousarray(np.concatenate(
            [xT[b][:, 512 * (4 * k + i):512 * (4 * k + i) + 512]
             for k in range(4)], axis=1))
        R = np.zeros((NG, 4, 128, 512), np.float32)
        for g in range(NG):
            s_k0 = 4 * (g // 4) + i
            if s_k0 > g:
                continue
            for kc in range(4):
                if s_k0 < g:
                    R[g, kc, 0, :] = 1.0
                else:
                    q = np.arange(512)
                    f = q - 128 * kc + 1
                    R[g, kc, 0, f <= 0] = 1.0
                    sel = (f >= 1) & (f <= 127)
                    R[g, kc, f[sel], q[sel]] = 1.0
        x_own_t = np.ascontiguousarray(
            x_own.reshape(8, 128, 2048).transpose(1, 0, 2))
        x_own_tb = np.ascontiguousarray(
            x_own_t.reshape(128, 8, 4, 512).transpose(2, 0, 1, 3))
        m = {"x_tiled": xtl[b], "x_own": x_own_t, "x_own_tb": x_own_tb,
             "rsel": _bf(R.transpose(0, 2, 1, 3))}
        m.update(shared)
        per_core.append(m)
    return per_core


class Prog:
    ENG = ("pe", "act", "dve", "sp", "gp")
    STREAMS = ("pe", "act", "dve", "sp", "gp", "sv", "sxo", "srs", "sx",
               "sab", "sab0", "sab1", "sabw0", "sabw1", "sabw2", "sabw3",
               "gpa", "gga", "gpp", "sabc0", "sabc1", "sabd0", "sabd1",
               "sx0", "sx1", "sx2", "sx3", "sgx0", "sgx1", "sag0", "sag1",
               "swk", "swk0", "swk1",
               "swst", "swst0", "swst1", "ssp", "sspg0", "sspg1", "st2s",
               "sout", "sout0", "sout1", "sw3r", "shp", "shpg", "shp0", "shp1",
               "sdump",
               "gq", "gwk", "gwv", "gbv", "gw1", "gw2", "gh1", "gh2", "gw3",
               "gwo")

    def __init__(self):
        self.ops = {e: [] for e in self.ENG}
        self.cnt = {s: 0 for s in self.STREAMS}

    def emit(self, eng, fn, waits=(), inc=None, stream=None):
        s = stream or eng
        if inc is None:
            inc = 16 if eng in ("sp", "gp") else 1
        self.ops[eng].append(([(e, v) for (e, v) in waits if v > 0], fn,
                              inc, s))
        self.cnt[s] += inc
        return self.cnt[s]

    def wait_only(self, eng, waits):
        self.ops[eng].append(([(e, v) for (e, v) in waits if v > 0], None,
                              0, eng))


CP = {"scale8": 0, "mexp": 1, "eps": 2, "mhalf": 3, "inv1024": 4,
      "mone": 5, "tiny": 6, "one": 7}
CPV = {"scale8": 0.125, "mexp": -8.0, "eps": 1e-5, "mhalf": -0.5,
       "inv1024": 1.0 / 1024.0, "mone": -1.0, "tiny": 1e-30, "one": 1.0}

_DEBUG_DUMPS = []
_PIPE_AV = True


def build_nc():
    nc = bass.Bass("TRN2", target_bir_lowering=False)

    dr = {}
    dr["x_tiled"] = nc.dram_tensor("x_tiled", [32, 128, 8, 256], F32,
                                   kind="ExternalInput")
    dr["x_own"] = nc.dram_tensor("x_own", [128, 8, TOK], F32,
                                 kind="ExternalInput")
    dr["x_own_tb"] = nc.dram_tensor("x_own_tb", [4, 128, 8, 512], F32,
                                    kind="ExternalInput")
    dr["rsel"] = nc.dram_tensor("rsel", [NG, 128, 4, 512], BF16,
                                kind="ExternalInput")
    for n, shp in [("wq", [128, 8, D]), ("wk", [8, 128, 8, 128]),
                   ("wv", [128, 8, D]), ("wo", [128, 8, D]),
                   ("w12t", [32, 128, 16, 128]), ("h1t", [32, 128, 32, 128]),
                   ("h2t", [32, 128, 32, 128]), ("w3t", [8, 128, 32, 128])]:
        dr[n] = nc.dram_tensor(n, shp, F32, kind="ExternalInput")
    for n in ["bq_sb", "bk_sb", "bo_sb", "s3_sb"]:
        dr[n] = nc.dram_tensor(n, [128, 8], F32, kind="ExternalInput")
    for n in ["s1_sb", "s2_sb", "sh1_sb", "sh2_sb"]:
        dr[n] = nc.dram_tensor(n, [128, 32], F32, kind="ExternalInput")
    dr["bv_row"] = nc.dram_tensor("bv_row", [1, D], F32, kind="ExternalInput")
    dr["Lmat"] = nc.dram_tensor("Lmat", [128, 128], BF16, kind="ExternalInput")
    dr["SEL"] = nc.dram_tensor("SEL", [128, 8, 128], F32,
                               kind="ExternalInput")
    dr["out"] = nc.dram_tensor("out", [4, 128, 8, 512], F32,
                               kind="ExternalOutput")
    dr["wk_bf"] = nc.dram_tensor("wk_bf", [8, 128, 8, 128], BF16)
    dr["w12t_bf"] = nc.dram_tensor("w12t_bf", [32, 128, 16, 128], BF16)
    dr["h1t_bf"] = nc.dram_tensor("h1t_bf", [32, 128, 32, 128], BF16)
    dr["h2t_bf"] = nc.dram_tensor("h2t_bf", [32, 128, 32, 128], BF16)
    dr["w3t_bf"] = nc.dram_tensor("w3t_bf", [8, 128, 32, 128], BF16)
    dr["t2sp"] = nc.dram_tensor("t2sp", [4, 128, 8, 512], F32)
    dr["ab"] = nc.dram_tensor("ab", [40, 512], F32)
    dr["ab2"] = nc.dram_tensor("ab2", [2, 1024], F32)
    dr["hp"] = nc.dram_tensor("hp", [32, 128, TOK], BF16)
    dr["hpp"] = nc.dram_tensor("hpp", [4, 128, 32, 512], BF16)

    sb = {}

    def T(name, shape, dtype, off):
        sb[name] = nc.alloc_sbuf_tensor_at(name, shape, dtype, offset=off)

    T("a_xown", [128, 8, TOK], F32, OFF_A)
    T("a_ctx", [128, 8, TOK], F32, OFF_A)
    T("a_x1", [128, 8, TOK], F32, OFF_A)
    T("a_ha", [128, 16, TOK], BF16, OFF_A)
    T("a_hppt", [128, 2, 32, 512], BF16, OFF_A)
    T("b_q", [128, 8, TOK], BF16, OFF_B)
    T("b_ctxfm", [128, 8, TOK], BF16, OFF_B)
    T("b_xln2", [128, 8, TOK], BF16, OFF_B)
    T("den", [128, 4, 512], F32, C_DEN)
    T("xg", [128, 8, 256], F32, C_XG)
    T("sq", [128, 8, 256], F32, C_SQ)
    T("tl", [128, 512], F32, C_TL)
    T("kg", [128, 8, 512], BF16, C_KG)
    T("vst", [128, 4, 16, 128], BF16, C_VST)
    T("pbuf", [128, 4, 512], BF16, C_P)
    T("pbuf2", [128, 4, 512], BF16, C_FREE)
    T("xln", [128, 8, 512], BF16, C_XLN)
    T("rselb", [128, 4, 512], BF16, C_RSEL)
    T("wkb", [128, 2, 8, 128], BF16, C_WK)
    T("wq_sb", [128, 8, 1024], BF16, D_WST)
    T("wo_sb", [128, 8, 1024], BF16, C_KG)
    T("sel_sb", [128, 8, 128], F32, OFF_C + 36864)
    T("lnd", [128, 4, 512], F32, C_XG)
    T("p_y", [128, 8, 512], F32, OFF_C + 0)
    T("p_xot", [128, 8, 512], F32, OFF_C + 40960)
    T("p_t2", [128, 8, 512], F32, OFF_C + 57344)
    T("p_sq", [128, 2, 512], F32, OFF_C + 16384)
    T("p_y2", [128, 8, 512], F32, OFF_C + 20480)
    T("c_hb", [128, 16, TOK], BF16, OFF_C)
    T("c_hp", [128, 2, TOK], BF16, C_FREE)
    T("c_g1", [128, 2, 512], BF16, C_FREE)
    T("rows", [1, 1024], F32, D_ROWS)
    T("rows_pp", [1, 2048], F32, C_VST)
    for _t in range(4):
        T(f"tlq{_t}", [128, 256], F32, OFF_B + 1024 * _t)
    T("gxb", [128, 8, 256], F32, D_WST)
    T("rselb2", [128, 4, 512], BF16, D_WST + 8192)
    T("absg", [128, 2, 512], F32, D_WST + 12288)
    T("absq", [128, 2, 512], F32, C_P)
    T("absc", [128, 2, 1024], F32, D_WST)
    T("absc2", [128, 2, 1024], F32, OFF_B)
    T("p_sq2", [128, 2, 512], F32, OFF_C + 36864)
    T("rows2", [1, 1024], F32, D_WV)
    T("p_sq3", [128, 2, 512], F32, D_WV + 8192)
    T("vstg", [128, 2, 512], BF16, C_SQ)
    T("w3r", [128, 3, 32, 128], BF16, OFF_B + 8192)
    T("xg4", [128, 8, 256], F32, C_VST + 8192)
    T("xg3", [128, 8, 256], F32, C_FREE)
    T("cpool", [128, 16], F32, D_CPOOL)
    T("bq", [128, 8], F32, D_BIAS + 0)
    T("bk", [128, 8], F32, D_BIAS + 32)
    T("bo", [128, 8], F32, D_BIAS + 64)
    T("s3", [128, 8], F32, D_BIAS + 96)
    T("s1", [128, 32], F32, D_SCAL + 0)
    T("s2", [128, 32], F32, D_SCAL + 128)
    T("sh1", [128, 32], F32, D_SCAL + 256)
    T("sh2", [128, 32], F32, D_SCAL + 384)
    T("onescol", [128, 1], F32, D_ONESCOL)
    T("onesrow", [1, 128], F32, D_ONESROW)
    T("onesrow_bf", [1, 128], BF16, D_ONESROW_BF)
    T("bv_bf", [1, 1024], BF16, D_BVROW)
    T("lmat", [128, 128], BF16, D_L)
    T("wst", [128, 2, 32, 128], BF16, D_WST)
    T("wv_sb", [128, 8, 1024], BF16, D_WV)

    psA = nc.alloc_psum_tensor("psA", [128, 2048], F32)
    psB = nc.alloc_psum_tensor("psB", [128, 2048], F32)
    ps = [psA, psB]

    P = Prog()
    M = {}

    def bank(t, j):
        return t[:, 512 * j:512 * (j + 1)]

    def _dump(name, ap, shape, dtype, waits):
        if name not in _DEBUG_DUMPS:
            return
        d = nc.dram_tensor("dbg_" + name, list(shape), dtype,
                           kind="ExternalOutput")
        def f(e, d=d, ap=ap):
            return e.dma_start(d.ap(), ap)
        v = P.emit("sp", f, waits=waits, stream="sdump")
        P.wait_only("sp", [("sdump", v)])

    def cp(name, p0=0, pn=128):
        return sb["cpool"][p0:p0 + pn, CP[name]:CP[name] + 1]

    rows = sb["rows"]

    def em_prologue():
        def f_memsets(v):
            for n, val in CPV.items():
                v.memset(sb["cpool"][:, CP[n]:CP[n] + 1], val)
            v.memset(sb["onescol"][:, :], 1.0)
            v.memset(sb["onesrow"][:, :], 1.0)
            return v.memset(sb["onesrow_bf"][:, :], 1.0)
        M["memsets"] = P.emit("dve", f_memsets)

        def dma(dst_ap, src_ap):
            def f(e, d=dst_ap, s=src_ap):
                return e.dma_start(d, s)
            return f
        for n, t in [("bq_sb", "bq"), ("bk_sb", "bk"), ("bo_sb", "bo"),
                     ("s3_sb", "s3"), ("s1_sb", "s1"), ("s2_sb", "s2"),
                     ("sh1_sb", "sh1"), ("sh2_sb", "sh2")]:
            P.emit("sp", dma(sb[t][:, :], dr[n][:, :]), stream="sv")
        P.emit("sp", dma(sb["lmat"][:, :], dr["Lmat"][:, :]), stream="sv")
        M["xown"] = P.emit("sp", dma(sb["a_xown"][:, :, :],
                                     dr["x_own"].ap()), stream="sxo")
        M[(0, "rsel")] = P.emit("sp", dma(sb["rselb"][:, :, :],
                                          dr["rsel"][0]), stream="srs")
        M["vecs"] = P.cnt["sv"]

        M["wq"] = P.emit("gp", dma(sb["wq_sb"][:, :, :], dr["wq"].ap()),
                         stream="gq")
        for cc in range(8):
            P.emit("gp", dma(dr["wk_bf"][cc], dr["wk"][cc]), stream="gwk")
        M["wk_bf"] = P.cnt["gwk"]
        M["wv"] = P.emit("gp", dma(sb["wv_sb"][:, :, :], dr["wv"].ap()),
                         stream="gwv")
        M["bv"] = P.emit("gp", dma(sb["bv_bf"][:, :], dr["bv_row"][:, :]),
                         stream="gbv")
    def mk_cast_list():
        def dma(dst_ap, src_ap):
            def f(e, d=dst_ap, s=src_ap):
                return e.dma_start(d, s)
            return f
        casts = []
        for hc in range(32):
            casts.append(("gw1", dma(dr["w12t_bf"][hc], dr["w12t"][hc])))
        for hc in range(32):
            casts.append(("gh1", dma(dr["h1t_bf"][hc], dr["h1t"][hc])))
        for hc in range(32):
            casts.append(("gh2", dma(dr["h2t_bf"][hc], dr["h2t"][hc])))
        for oc in range(8):
            casts.append(("gw3", dma(dr["w3t_bf"][oc], dr["w3t"][oc])))
        return casts

    def em_casts(casts, n, gate):
        first = True
        while casts and n > 0:
            strm, f = casts.pop(0)
            P.emit("gp", f, stream=strm, waits=gate if first else ())
            first = False
            n -= 1
        if not casts:
            M["w12t_bf"] = P.cnt["gw1"]
            M["h1t_bf"] = P.cnt["gh1"]
            M["h2t_bf"] = P.cnt["gh2"]
            M["w3t_bf"] = P.cnt["gw3"]

    def em_stats_chain(xa, ntok, tag, sq_t, extra_pe=(), extra_dve=(),
                       extra_ld=(), abuf=None):
        ci0 = M.get("abc_i", -1) + 1
        rbuf = sb["rows"] if ci0 % 2 == 0 else sb["rows2"]
        sqb = sq_t if ci0 % 2 == 0 else sb["p_sq3"]
        base = 1024 if ci0 % 2 == 0 else 0
        """xa: 8 APs [128, ntok] f32.  sq_t: [128, 2, ntok] f32 ping buffer.
        Produces a-row at rows[512:512+ntok], b-row at rows[0:ntok], and
        broadcasts a_b -> psB[:, 0:ntok], b_b -> psB[:, 512:512+ntok].
        Returns pe value after broadcasts."""
        def f_sx(t):
            r = None
            for c in range(8):
                r = t.matmul(psB[0:1, base:base + ntok], sb["onescol"][:, :],
                             xa[c], start=(c == 0), stop=(c == 7))
            return r
        pe_sx = P.emit("pe", f_sx, waits=list(extra_pe))
        M[(tag, "sx_pe")] = pe_sx
        last_pe = None
        for c in range(8):
            pg = c % 2
            if c < 2:
                wd = [("pe", M.get(("sx2t", ci0 % 2), 0))] + list(extra_dve)
            else:
                wd = [("pe", M.get((tag, "sx2", c - 2), 0))] + list(extra_dve)
            def f_sqc(v, c=c, pg=pg):
                return v.tensor_mul(sqb[:, pg, :], xa[c], xa[c])
            d_sq = P.emit("dve", f_sqc, waits=wd)
            def f_sx2(t, c=c, pg=pg):
                return t.matmul(psB[0:1, base + 512:base + 512 + ntok],
                                sb["onescol"][:, :], sqb[:, pg, :],
                                start=(c == 0), stop=(c == 7))
            last_pe = P.emit("pe", f_sx2, waits=[("dve", d_sq)])
            M[(tag, "sx2", c)] = last_pe
        M[("sx2t", ci0 % 2)] = last_pe

        def f_mu(v):
            return v.tensor_scalar(rbuf[0:1, 0:ntok],
                                   psB[0:1, base:base + ntok],
                                   cp("inv1024", 0, 1), None, ALU.mult)
        P.emit("dve", f_mu,
               waits=[("pe", pe_sx),
                      (f"sabc{ci0 % 2}", M.get(f"abcw{ci0 % 2}", 0))])

        def f_mu2(v):
            return v.tensor_mul(rbuf[0:1, 512:512 + ntok], rbuf[0:1, 0:ntok],
                                rbuf[0:1, 0:ntok])
        P.emit("dve", f_mu2)

        def f_var(v):
            return v.scalar_tensor_tensor(
                rbuf[0:1, 512:512 + ntok], psB[0:1, base + 512:base + 512 + ntok],
                cp("inv1024", 0, 1), rbuf[0:1, 512:512 + ntok],
                ALU.mult, ALU.subtract)
        d_var = P.emit("dve", f_var, waits=[("pe", last_pe)])

        def f_ln(s):
            return s.activation(rbuf[0:1, 512:512 + ntok],
                                rbuf[0:1, 512:512 + ntok], AF.Ln,
                                bias=cp("eps", 0, 1))
        P.emit("act", f_ln, waits=[("dve", d_var)])

        def f_exp(s):
            return s.activation(rbuf[0:1, 512:512 + ntok],
                                rbuf[0:1, 512:512 + ntok], AF.Exp,
                                scale=cp("mhalf", 0, 1))
        a_a = P.emit("act", f_exp)

        def f_b(v):
            return v.scalar_tensor_tensor(rbuf[0:1, 0:ntok], rbuf[0:1, 0:ntok],
                                          cp("mone", 0, 1),
                                          rbuf[0:1, 512:512 + ntok],
                                          ALU.mult, ALU.mult)
        d_b = P.emit("dve", f_b, waits=[("act", a_a)])

        ci = M["abc_i"] = M.get("abc_i", -1) + 1
        pg2 = ci % 2
        def f_w(e, pg2=pg2):
            return e.dma_start(dr["ab2"][pg2:pg2 + 1, :], rbuf[0:1, 0:1024])
        wcnt = P.emit("sp", f_w, waits=[("dve", d_b)], stream=f"sabc{pg2}")
        M[f"abcw{pg2}"] = wcnt
        if abuf is None:
            abuf = sb["absc"]
        def f_l(e, pg2=pg2, abuf=abuf):
            return e.dma_start(
                abuf[:, pg2, :],
                dr["ab2"].ap().partition_broadcast(128)[:, pg2, :])
        wl = [(f"sabc{pg2}", wcnt),
              ("dve", M.get(("abs_use", ci - 2), 0))] + list(extra_ld)
        ld = P.emit("sp", f_l, waits=wl, stream=f"sabd{pg2}")
        M[(tag, "abs")] = (pg2, ld, ci)
        if tag == ("pa", 0):
            _dump("absc0", sb["absc"][:, pg2, :], [128, 1024], F32,
                  [(f"sabd{pg2}", ld)])
            _dump("rowspa0", rbuf[0:1, 0:1024], [1, 1024], F32,
                  [(f"sabc{pg2}", wcnt)])
        return (f"sabd{pg2}", ld, pg2, ci)

    def em_chain(i):
        par = i % 4
        pa2 = i % 2
        mu_o = 512 * par
        va_o = 512 * par + 256
        rpp = sb["rows_pp"]
        xg2 = sb["den"][:, :, :].rearrange("p a b -> p (a b)").rearrange(
            "p (c t) -> p c t", c=8)
        if i < 8:
            xa = [sb["a_xown"][:, c, 256 * i:256 * (i + 1)]
                  for c in range(8)]
            sp_x = ("sxo", M["xown"])
        else:
            xgt = [sb["xg"], xg2, sb["xg3"], sb["xg4"]][par]
            sp_x = (f"sx{par}", M[("xload", i)])
            xa = [xgt[:, c, :] for c in range(8)]
        def f_sx(t, xa=xa, pa2=pa2):
            r = None
            for c in range(8):
                r = t.matmul(psA[0:1, 512 * pa2:512 * pa2 + 256],
                             sb["onescol"][:, :], xa[c],
                             start=(c == 0), stop=(c == 7))
            return r
        pe_sx = P.emit("pe", f_sx,
                       waits=[sp_x, ("dve", M["memsets"]),
                              ("dve", M.get(("pp_mu", i - 2), 0))])
        M[("pp_sxe", i)] = pe_sx
        last_pe = None
        d_sq = None
        for c in range(8):
            pg = c
            wd = [("pe", M.get(("pp_sq", i - 1, c), 0))]
            if c == 0:
                wd.append(sp_x)
            def f_sq(v, xac=xa[c], pg=pg):
                return v.tensor_mul(sb["sq"][:, pg, :], xac, xac)
            d_sq = P.emit("dve", f_sq, waits=wd)
            w_m2 = [("dve", d_sq)]
            if c == 0:
                w_m2.append(("dve", M.get(("pp_var", i - 2), 0)))
            def f_sx2(t, c=c, pg=pg, pa2=pa2):
                return t.matmul(psB[0:1, 512 * pa2:512 * pa2 + 256],
                                sb["onescol"][:, :], sb["sq"][:, pg, :],
                                start=(c == 0), stop=(c == 7))
            last_pe = P.emit("pe", f_sx2, waits=w_m2)
            M[("pp_sq", i, c)] = last_pe
        M[("pp_sx2", i)] = last_pe
        M[("pp_dsq", i)] = d_sq
        def f_mu(v, mu_o=mu_o, pa2=pa2):
            return v.tensor_scalar(rpp[0:1, mu_o:mu_o + 256],
                                   psA[0:1, 512 * pa2:512 * pa2 + 256],
                                   cp("inv1024", 0, 1), None, ALU.mult)
        M[("pp_mu", i)] = P.emit(
            "dve", f_mu,
            waits=[("pe", pe_sx),
                   (f"sabw{par}", M.get(("pp_ab", i - 4), 0))])
        def f_mu2(v, mu_o=mu_o, va_o=va_o):
            return v.tensor_mul(rpp[0:1, va_o:va_o + 256],
                                rpp[0:1, mu_o:mu_o + 256],
                                rpp[0:1, mu_o:mu_o + 256])
        P.emit("dve", f_mu2)
        def f_var(v, va_o=va_o, pa2=pa2):
            return v.scalar_tensor_tensor(
                rpp[0:1, va_o:va_o + 256],
                psB[0:1, 512 * pa2:512 * pa2 + 256],
                cp("inv1024", 0, 1), rpp[0:1, va_o:va_o + 256],
                ALU.mult, ALU.subtract)
        M[("pp_var", i)] = P.emit("dve", f_var, waits=[("pe", last_pe)])
        def f_ln(s, va_o=va_o):
            return s.activation(rpp[0:1, va_o:va_o + 256],
                                rpp[0:1, va_o:va_o + 256], AF.Ln,
                                bias=cp("eps", 0, 1))
        P.emit("act", f_ln, waits=[("dve", M[("pp_var", i)])])
        def f_exp(s, va_o=va_o):
            return s.activation(rpp[0:1, va_o:va_o + 256],
                                rpp[0:1, va_o:va_o + 256], AF.Exp,
                                scale=cp("mhalf", 0, 1))
        a_a = P.emit("act", f_exp)
        def f_b(v, mu_o=mu_o, va_o=va_o):
            return v.scalar_tensor_tensor(
                rpp[0:1, mu_o:mu_o + 256], rpp[0:1, mu_o:mu_o + 256],
                cp("mone", 0, 1), rpp[0:1, va_o:va_o + 256],
                ALU.mult, ALU.mult)
        M[("pp_b", i)] = P.emit("dve", f_b, waits=[("act", a_a)])
        def f_ab(e, i=i, par=par):
            return e.dma_start(dr["ab"][i:i + 1, :],
                               rpp[0:1, 512 * par:512 * par + 512])
        M[("pp_ab", i)] = P.emit("sp", f_ab, stream=f"sabw{par}",
                                 waits=[("dve", M[("pp_b", i)])])

    def em_xload(i):
        par = i % 4
        xg2 = sb["den"][:, :, :].rearrange("p a b -> p (a b)").rearrange(
            "p (c t) -> p c t", c=8)
        xgt = [sb["xg"], xg2, sb["xg3"], sb["xg4"]][par]
        w_ld = [("pe", M.get(("pp_sxe", i - 4), 0)),
                ("dve", M.get(("pp_dsq", i - 4), 0))]
        def f_ld(e, s=dr["x_tiled"][i - 8], xgt=xgt):
            return e.dma_start(xgt[:, :, :], s)
        M[("xload", i)] = P.emit("sp", f_ld, waits=w_ld, stream=f"sx{par}")

    def em_qab(idx):
        w = [(f"sabw{idx % 4}", M[("pp_ab", idx)]),
             ("gpa", M.get(("q_apply", idx - 2), 0))]
        def f_ab(e, idx=idx):
            return e.dma_start(
                sb["absq"][:, idx % 2, :],
                dr["ab"].ap().partition_broadcast(128)[:, idx, :])
        strm = "sab0" if idx % 2 == 0 else "sab1"
        M[("qab", idx)] = P.emit("sp", f_ab, waits=w, stream=strm)

    def em_qapply(t, j):
        idx = 2 * t + j
        strm = "sab0" if idx % 2 == 0 else "sab1"
        tq = sb[f"tlq{t}"]
        d_apply = None
        for c in range(8):
            xac = sb["a_xown"][:, c, 256 * idx:256 * (idx + 1)]
            wt = [(strm, M[("qab", idx)]), ("sxo", M["xown"])]
            if c == 0 and j == 0:
                wt.append(("pe", M.get(("qproj_pe", t - 1), 0)))
            def f_t(v, xac=xac, tq=tq, idx=idx):
                return v.tensor_mul(tq[:, :], xac,
                                    sb["absq"][:, idx % 2, 256:512])
            P.emit("gp", f_t, waits=wt, stream="gpa", inc=1)
            def f_x(v, c=c, j=j, tq=tq, idx=idx):
                return v.tensor_add(
                    sb["xln"][:, c, 256 * j:256 * (j + 1)],
                    tq[:, :], sb["absq"][:, idx % 2, 0:256])
            d_apply = P.emit("gp", f_x, stream="gpa", inc=1)
        M[("q_apply", idx)] = d_apply

    def em_qproj(t):
        d_apply = M[("q_apply", 2 * t + 1)]
        ga = [("gpa", d_apply)]
        if t == 0:
            _dump("xlnt0", sb["xln"][:, :, :], [128, 8, 512], BF16,
                  [("gpa", d_apply)])
        for cc in range(8):
            pg = cc % 2
            fl = 8 * t + cc
            def f_mm(t_, cc=cc, pg=pg):
                r = None
                for dc in range(8):
                    r = t_.matmul(bank(psA, 2 + pg),
                                  sb["wq_sb"][:, dc,
                                              128 * cc:128 * cc + 128],
                                  sb["xln"][:, dc, :],
                                  start=(dc == 0), stop=(dc == 7))
                return r
            w = ga + [("gq", M["wq"]),
                 ("act", M.get(("qt_tanh", fl - 2), 0)), ("sv", M["vecs"])]
            if _DEBUG_DUMPS:
                w += [("sdump", P.cnt["sdump"])]
            pe_q = P.emit("pe", f_mm, waits=w)
            M[("qproj_pe", t)] = pe_q
            def f_tanh(s, t=t, cc=cc, pg=pg):
                return s.activation(
                    sb["b_q"][:, cc, 512 * t:512 * (t + 1)],
                    bank(psA, 2 + pg), AF.Tanh, bias=sb["bq"][:, cc:cc + 1])
            M[("qt_tanh", fl)] = P.emit("act", f_tanh, waits=[("pe", pe_q)])

    def em_qfinish():
        M["q_done_act"] = P.cnt["act"]
        M["q_done_pe"] = P.cnt["pe"]

        def f_vst(v):
            v.memset(sb["vst"][:, :, :, :], 0.0)
            r = None
            for h in range(16):
                j = h if (h % 2 == 1) else 64 + h
                r = v.memset(sb["vst"][:, :, h, j:j + 1], 1.0)
            return r
        M["vst_init"] = P.emit(
            "dve", f_vst,
            waits=[("pe", M["q_done_pe"]), ("act", M["q_done_act"]),
                   ("pe", M[("pp_sx2", 39)]),
                   ("sabw0", M[("pp_ab", 36)]),
                   ("sabw1", M[("pp_ab", 37)]),
                   ("sabw2", M[("pp_ab", 38)]),
                   ("sabw3", M[("pp_ab", 39)])])

    vstate = {"v": 0}
    E_hist, EA_hist, D_hist = {}, {}, {}

    def em_group_ln(g):
        sp_xs = []
        for j in range(2):
            H = 2 * g + j
            if g == 0:
                w_load = [("dve", M["memsets"]),
                          ("pe", M.get(("pp_sx2", 39), 0)),
                          ("dve", M[("pp_b", 39)])]
            else:
                w_load = [("gga", M[(g - 1, j, "apply_last")])]
            buf = [sb["xg"], sb["gxb"]][H % 2]
            def f_ld(e, s=dr["x_tiled"][H], buf=buf):
                return e.dma_start(buf[:, :, :], s)
            sp_xs.append(P.emit("sp", f_ld, waits=w_load,
                                stream=f"sgx{H % 2}"))
        d_apply = None
        for j in range(2):
            idx = 8 + 2 * g + j
            H = 2 * g + j
            buf = [sb["xg"], sb["gxb"]][H % 2]
            wab = [(f"sabw{idx % 4}", M[("pp_ab", idx)]),
                   ("gga", M.get((g - 1, j, "apply_last"), 0))]
            def f_ab(e, idx=idx, j=j):
                return e.dma_start(
                    sb["absg"][:, j, :],
                    dr["ab"].ap().partition_broadcast(128)[:, idx, :])
            strm = "sag0" if j == 0 else "sag1"
            sp_ab = P.emit("sp", f_ab, waits=wab, stream=strm)
            for c in range(8):
                wt = [(strm, sp_ab), (f"sgx{H % 2}", sp_xs[j])]
                if c == 0 and j == 0:
                    wt.append(("pe", M.get((g - 1, "kv_last"),
                                           M.get(("pp_sx2", 39), 0))))
                def f_t(v, c=c, buf=buf, j=j):
                    return v.tensor_mul(sb["tl"][:, 0:256], buf[:, c, :],
                                        sb["absg"][:, j, 256:512])
                P.emit("gp", f_t, waits=wt, stream="gga", inc=1)
                def f_x(v, c=c, j=j):
                    return v.tensor_add(
                        sb["xln"][:, c, 256 * j:256 * (j + 1)],
                        sb["tl"][:, 0:256], sb["absg"][:, j, 0:256])
                d_apply = P.emit("gp", f_x, stream="gga", inc=1)
            M[(g, j, "apply_last")] = d_apply
        M[(g, "apply_last")] = d_apply

    def em_group_kv(g):
        apply_done = M[(g, "apply_last")]
        kv_first_act = P.cnt["act"]
        for cc in range(8):
            pgw = cc % 2
            fl = 8 * g + cc
            def f_wk(e, cc=cc, pgw=pgw):
                return e.dma_start(sb["wkb"][:, pgw, :, :], dr["wk_bf"][cc])
            sp_w = P.emit("sp", f_wk, stream=f"swk{fl % 2}",
                          waits=[("pe", M.get(("kmmf", fl - 2), 0)),
                                 ("gwk", M["wk_bf"])])
            pg = 2 + (cc % 2)
            def f_mm(t_, cc=cc, pg=pg, pgw=pgw):
                r = None
                for dc in range(8):
                    r = t_.matmul(bank(psB, pg), sb["wkb"][:, pgw, dc, :],
                                  sb["xln"][:, dc, :],
                                  start=(dc == 0), stop=(dc == 7))
                return r
            pe_k = P.emit("pe", f_mm,
                          waits=[(f"swk{fl % 2}", sp_w),
                                 ("gga", apply_done),
                                 ("act", M.get((g, "ktanh", cc - 1), 0))])
            M[("kmmf", fl)] = pe_k
            def f_tanh(s, cc=cc, pg=pg):
                return s.activation(sb["kg"][:, cc, :], bank(psB, pg),
                                    AF.Tanh, bias=sb["bk"][:, cc:cc + 1])
            M[(g, "ktanh", cc)] = P.emit("act", f_tanh, waits=[("pe", pe_k)])

        vc_last = 0
        for kc in range(4):
            for cb in range(2):
                pg = (2 * kc + cb) % 2
                def f_mm(t_, kc=kc, cb=cb, pg=pg):
                    r = t_.matmul(bank(psB, pg), sb["onesrow_bf"][:, :],
                                  sb["bv_bf"][0:1, 512 * cb:512 * (cb + 1)],
                                  start=True, stop=False)
                    for dc in range(8):
                        r = t_.matmul(
                            bank(psB, pg),
                            sb["xln"][:, dc, 128 * kc:128 * (kc + 1)],
                            sb["wv_sb"][:, dc, 512 * cb:512 * (cb + 1)],
                            start=False, stop=(dc == 7))
                    return r
                w_mm = [("gwv", M["wv"]), ("gbv", M["bv"]),
                        ("gga", apply_done),
                        ("act", M.get((g, "ktanh", 7), 0)),
                        ("act", M.get((g, "vtanh", 2 * kc + cb - 1), 0))]
                pe_v = P.emit("pe", f_mm, waits=w_mm)
                def f_tanh(s, cb=cb, pg=pg):
                    return s.activation(sb["vstg"][:, cb, :], bank(psB, pg),
                                        AF.Tanh)
                w_t = [("pe", pe_v),
                       ("dve", M.get((g, "vcopy", kc - 1, cb), 0))]
                a_v = P.emit("act", f_tanh, waits=w_t)
                M[(g, "vtanh", 2 * kc + cb)] = a_v
                def f_cp(v, kc=kc, cb=cb):
                    r = None
                    for hh in range(8):
                        h = cb * 8 + hh
                        r = v.tensor_copy(
                            sb["vst"][:, kc, h,
                                      64 * (h % 2):64 * (h % 2) + 64],
                            sb["vstg"][:, cb, 64 * hh:64 * hh + 64])
                    return r
                vc_last = P.emit("dve", f_cp, waits=[("act", a_v)])
                M[(g, "vcopy", kc, cb)] = vc_last
        M[(g, "kv_last")] = P.cnt["pe"]
        M[(g, "vcopy_last")] = vc_last
        M[(g, "kv_act_last")] = P.cnt["act"]

    def em_avd(g, v, h, k, psX, pbt):
        def f_AV(t_, h=h, psX=psX, pbt=pbt):
            r = None
            for kc in range(4):
                r = t_.matmul(bank(psX, 0), sb["vst"][:, kc, h, :],
                              pbt[:, kc, :],
                              start=(kc == 0), stop=(kc == 3))
            return r
        wA = [("act", E_hist[v])]
        if v == M.get(("gstart", g), -1):
            wA.append(("dve", M[(g, "vcopy_last")]))
        A_v = P.emit("pe", f_AV, waits=wA)
        first_ctx = (g == 0)
        first_den = (g == 0 and h < 2)
        def f_D(vv, h=h, k=k, psX=psX, first_ctx=first_ctx,
                first_den=first_den):
            pb = 64 * (h % 2)
            cc = h // 2
            ctx_dst = sb["a_ctx"][pb:pb + 64, cc, 512 * k:512 * (k + 1)]
            den_dst = sb["den"][64 - pb:128 - pb, k, :]
            ctx_src = psX[pb:pb + 64, 0:512]
            den_src = psX[64 - pb:128 - pb, 0:512]
            if first_ctx:
                vv.tensor_copy(ctx_dst, ctx_src)
            else:
                vv.tensor_add(ctx_dst, ctx_dst, ctx_src)
            if first_den:
                return vv.tensor_copy(den_dst, den_src)
            return vv.tensor_add(den_dst, den_dst, den_src)
        D_hist[v] = P.emit("dve", f_D, waits=[("pe", A_v)])

    def em_visits(g):
        k0 = g // 4
        slots = [k for k in range(k0 + 1, 4)] + [k0]
        M[("gstart", g)] = vstate["v"]
        pend = None
        nvis = 0
        inject_at = 16 * len(slots) // 2
        for k in slots:
            diag = (k == k0)
            for h in range(16):
                if nvis == inject_at and g + 1 < NG:
                    em_group_ln(g + 1)
                nvis += 1
                v = vstate["v"]
                psX = ps[v % 2]
                pbt = [sb["pbuf"], sb["pbuf2"]][v % 2]
                wS = [("act", EA_hist.get(v - 2, 0))]
                if v % 2 == 0:
                    wS.append(("act", M[(g, "ktanh", h // 2)]))
                else:
                    wS.append(("act", M[(g, "kv_act_last")]))
                if diag:
                    wS.append(("srs", M[(g, "rsel")]))
                rsb = sb["rselb"] if g % 2 == 0 else sb["rselb2"]
                def f_Sk(t_, k=k, h=h, diag=diag, psX=psX, rsb=rsb,
                         kcs=(1, 2, 3)):
                    cc, pb = h // 2, 64 * (h % 2)
                    r = None
                    for kc in kcs:
                        r = t_.matmul(
                            bank(psX, kc),
                            sb["kg"][pb:pb + 64, cc, 128 * kc:128 * (kc + 1)],
                            sb["b_q"][pb:pb + 64, cc, 512 * k:512 * (k + 1)],
                            start=True, stop=(not diag))
                        if diag:
                            r = t_.matmul(bank(psX, kc), sb["lmat"][:, :],
                                          rsb[:, kc, :],
                                          start=False, stop=True)
                    return r
                Sa_v = P.emit("pe", f_Sk, waits=wS)
                wEa = [("pe", Sa_v)]
                if v == 1:
                    wEa += [("pe", M[("pp_sx2", 38)]),
                            ("dve", M[("pp_dsq", 38)])]


                def f_Ea(s, psX=psX, pbt=pbt):
                    return s.activation(
                        pbt[:, 1:4, :],
                        psX[:, 512:2048].rearrange("p (a b) -> p a b", a=3),
                        AF.Exp, bias=cp("mexp"), scale=cp("scale8"))
                EA_hist[v] = P.emit("act", f_Ea, waits=wEa)
                if pend is not None:
                    em_avd(*pend)
                    pend = None
                def f_Sb(t_, k=k, h=h, diag=diag, psX=psX):
                    return f_Sk(t_, k=k, h=h, diag=diag, psX=psX, kcs=(0,))
                S_v = P.emit("pe", f_Sb,
                             waits=[("dve", D_hist.get(v - 2, 0))])
                def f_Eb(s, psX=psX, pbt=pbt):
                    return s.activation(pbt[:, 0, :], psX[:, 0:512],
                                        AF.Exp, bias=cp("mexp"),
                                        scale=cp("scale8"))
                E_hist[v] = P.emit("act", f_Eb, waits=[("pe", S_v)])
                pend = (g, v, h, k, psX, pbt)
                vstate["v"] += 1
        if _PIPE_AV:
            em_avd(*pend)
        if g + 1 < NG:
            dst = sb["rselb2"] if (g + 1) % 2 == 1 else sb["rselb"]
            def f_rs(e, g=g, dst=dst):
                return e.dma_start(dst[:, :, :], dr["rsel"][g + 1])
            M[(g + 1, "rsel")] = P.emit(
                "sp", f_rs, stream="srs",
                waits=[("pe", M.get((g - 1, "vend_pe"), 0))])

    def em_postattn():
        last_v = vstate["v"] - 1
        drain = [("dve", D_hist[last_v]), ("act", E_hist[last_v])]
        def f_sel(e):
            return e.dma_start(sb["sel_sb"][:, :, :],
                               dr["SEL"].ap())
        sp_sel = P.emit("sp", f_sel, waits=drain, stream="sv")
        def f_wo(e):
            return e.dma_start(
                sb["wo_sb"][:, :, :],
                dr["wo"].ap())
        gp_wo = P.emit("gp", f_wo, waits=drain, stream="gwo")
        def f_ln(s):
            return s.activation(sb["lnd"][:, :, :], sb["den"][:, :, :],
                                AF.Ln, bias=cp("tiny"))
        w_ln = list(drain)
        if _DEBUG_DUMPS:
            w_ln += [("sdump", P.cnt["sdump"])]
        P.emit("act", f_ln, waits=w_ln)
        def f_ex(s):
            return s.activation(sb["den"][:, :, :], sb["lnd"][:, :, :],
                                AF.Exp, scale=cp("mone"))
        a2 = P.emit("act", f_ex)
        lastd = 0
        for c in range(8):
            for k in range(4):
                fl = 4 * c + k
                pg = fl % 2
                def f_mm(t_, c=c, k=k, pg=pg):
                    return t_.matmul(bank(psA, pg), sb["sel_sb"][:, c, :],
                                     sb["den"][:, k, :], start=True,
                                     stop=True)
                pe_m = P.emit("pe", f_mm,
                              waits=[("act", a2), ("sv", sp_sel),
                                     ("dve", M.get(("ctxn", fl - 2), 0))])
                def f_mul(vv, c=c, k=k, pg=pg):
                    return vv.tensor_mul(
                        sb["b_ctxfm"][:, c, 512 * k:512 * (k + 1)],
                        sb["a_ctx"][:, c, 512 * k:512 * (k + 1)],
                        bank(psA, pg))
                lastd = P.emit("dve", f_mul, waits=[("pe", pe_m)])
                M[("ctxn", fl)] = lastd
        M["ctxn_last"] = lastd
        _dump("rdn", sb["den"][:, :, :], [128, 4, 512], F32, [("act", a2)])
        _dump("ctxfm", sb["b_ctxfm"][:, :, :], [128, 8, TOK], BF16,
              [("dve", lastd)])

        for tb in range(4):
            def f_xo(e, tb=tb):
                return e.dma_start(sb["p_xot"][:, :, :], dr["x_own_tb"][tb])
            sp_xo = P.emit("sp", f_xo, stream="sxo",
                           waits=[("dve", M.get(("pa_xo", tb - 1), 0))])
            for oc in range(8):
                def f_mm(t_, tb=tb, oc=oc):
                    r = None
                    for ccc in range(8):
                        r = t_.matmul(
                            bank(psA, oc % 4),
                            sb["wo_sb"][:, ccc, 128 * oc:128 * oc + 128],
                            sb["b_ctxfm"][:, ccc, 512 * tb:512 * (tb + 1)],
                            start=(ccc == 0), stop=(ccc == 7))
                    return r
                w = [("gwo", gp_wo), ("dve", M["ctxn_last"])]
                if oc >= 4:
                    w.append(("act", M.get(("pa_tanh", tb, oc - 4), 0)))
                else:
                    w.append(("act", M.get(("pa_tanh", tb - 1, oc + 4), 0)))
                if _DEBUG_DUMPS:
                    w += [("sdump", P.cnt["sdump"])]
                pe_o = P.emit("pe", f_mm, waits=w)
                wt = [("pe", pe_o),
                      ("dve", M.get(("pa_ft", tb - 1, oc), 0)),
                      ("pe", M.get((("pa", tb - 1), "sx_pe"), 0))]
                def f_tanh(s, oc=oc):
                    return s.activation(sb["p_y"][:, oc, :],
                                        bank(psA, oc % 4), AF.Tanh,
                                        bias=sb["bo"][:, oc:oc + 1])
                M[("pa_tanh", tb, oc)] = P.emit("act", f_tanh, waits=wt)
            if tb == 0:
                _dump("y0", sb["p_y"][:, :, :], [128, 8, 512], F32,
                      [("act", M[("pa_tanh", 0, 7)])])
            ya = [sb["p_y"][:, c, :] for c in range(8)]
            tw = [("act", M[("pa_tanh", tb, 7)])]
            sd, ld, apg, ci = em_stats_chain(ya, 512, ("pa", tb), sb["p_sq"],
                                             extra_pe=tw, extra_dve=tw)
            d_last = None
            for c in range(8):
                def f_t(vv, c=c, apg=apg):
                    return vv.tensor_mul(sb["p_sq2"][:, c % 2, :],
                                         sb["p_y"][:, c, :],
                                         sb["absc"][:, apg, 512:1024])
                M[("pa_ft", tb, c)] = P.emit("dve", f_t, waits=[(sd, ld)])
                def f_t2(vv, c=c, apg=apg):
                    return vv.tensor_add(sb["p_t2"][:, c, :],
                                         sb["p_sq2"][:, c % 2, :],
                                         sb["absc"][:, apg, 0:512])
                P.emit("dve", f_t2)
                def f_x1(vv, c=c, tb=tb):
                    return vv.tensor_add(
                        sb["a_x1"][:, c, 512 * tb:512 * (tb + 1)],
                        sb["p_xot"][:, c, :], sb["p_t2"][:, c, :])
                d_last = P.emit("dve", f_x1, waits=[("sxo", sp_xo)])
            M[("pa_xo", tb)] = d_last
            M[("abs_use", ci)] = d_last
            def f_sp_t2(e, tb=tb):
                return e.dma_start(dr["t2sp"][tb], sb["p_t2"][:, :, :])
            M[("t2spill", tb)] = P.emit("gp", f_sp_t2, stream="st2s",
                                        waits=[("dve", d_last)])
            x1a = [sb["a_x1"][:, c, 512 * tb:512 * (tb + 1)]
                   for c in range(8)]
            tw3 = [("dve", M[("pa_xo", tb)])]
            sd3, ld3, apg3, ci3 = em_stats_chain(x1a, 512, ("l3", tb),
                                                 sb["p_sq"], extra_pe=tw3,
                                                 extra_dve=tw3)
            d3 = None
            for c in range(8):
                def f_t3(vv, c=c, x1a=x1a, apg3=apg3):
                    return vv.tensor_mul(sb["p_sq2"][:, c % 2, :], x1a[c],
                                         sb["absc"][:, apg3, 512:1024])
                P.emit("dve", f_t3, waits=[(sd3, ld3)])
                def f_x3(vv, c=c, tb=tb, apg3=apg3):
                    return vv.tensor_add(
                        sb["b_xln2"][:, c, 512 * tb:512 * (tb + 1)],
                        sb["p_sq2"][:, c % 2, :],
                        sb["absc"][:, apg3, 0:512])
                d3 = P.emit("dve", f_x3)
            M[("abs_use", ci3)] = d3
        M["xln2_done"] = P.cnt["dve"]

    def em_ffn():
        for hc in range(32):
            pgw = hc % 2
            def f_w1(e, hc=hc, pgw=pgw):
                return e.dma_start(sb["wst"][:, pgw, 0:16, :],
                                   dr["w12t_bf"][hc])
            w_ld = [("pe", M.get(("s1mm", 4 * (hc - 2) + 3), 0)),
                    ("gw1", M["w12t_bf"])]
            if hc < 2:
                w_ld += [("act", M["q_done_act"]),
                         ("dve", M["xln2_done"]),
                         ("gga", M[(NG - 1, "apply_last")])]
            sp2 = P.emit("sp", f_w1, waits=w_ld, stream=f"swst{hc % 2}")
            for tb in range(4):
                fl = 4 * hc + tb
                r0 = 2 * (fl % 4)
                pst1 = psA if r0 < 4 else psB
                pst2 = psA if (r0 + 1) < 4 else psB
                b1, b2 = r0 % 4, (r0 + 1) % 4
                def f_mm(t_, tb=tb, pgw=pgw, pst1=pst1, pst2=pst2,
                         b1=b1, b2=b2):
                    r = None
                    for dc in range(8):
                        r = t_.matmul(bank(pst1, b1), sb["wst"][:, pgw, dc, :],
                                      sb["b_xln2"][:, dc,
                                                   512 * tb:512 * (tb + 1)],
                                      start=(dc == 0), stop=(dc == 7))
                    for dc in range(8):
                        r = t_.matmul(bank(pst2, b2),
                                      sb["wst"][:, pgw, 8 + dc, :],
                                      sb["b_xln2"][:, dc,
                                                   512 * tb:512 * (tb + 1)],
                                      start=(dc == 0), stop=(dc == 7))
                    return r
                w_mm = [(f"swst{hc % 2}", sp2), ("dve", M["xln2_done"]),
                        ("dve", M.get(("s1dve", fl - 4), 0)),
                        ("act", M.get(("s1act", fl - 4), 0))]
                if fl == 0:
                    w_mm += [("st2s", M[("t2spill", 3)])]
                    if _DEBUG_DUMPS:
                        w_mm += [("sdump", P.cnt["sdump"])]
                pe_1 = P.emit("pe", f_mm, waits=w_mm)
                M[("s1mm", fl)] = pe_1
                pga = tb % 2
                def f_silu(s, hc=hc, pga=pga, pst1=pst1, b1=b1):
                    return s.activation(sb["c_g1"][:, pga, :], bank(pst1, b1),
                                        AF.Silu, scale=sb["s1"][:, hc:hc + 1])
                a_s = P.emit("act", f_silu,
                             waits=[("pe", pe_1),
                                    ("dve", M.get(("s1dve", fl - 2), 0))])
                M[("s1act", fl)] = a_s
                hv = sb["a_ha"] if hc < 16 else sb["c_hb"]
                def f_h(vv, hc=hc, tb=tb, pga=pga, hv=hv, pst2=pst2, b2=b2):
                    return vv.scalar_tensor_tensor(
                        hv[:, hc % 16, 512 * tb:512 * (tb + 1)],
                        bank(pst2, b2), sb["s2"][:, hc:hc + 1],
                        sb["c_g1"][:, pga, :], ALU.mult, ALU.mult)
                M[("s1dve", fl)] = P.emit("dve", f_h, waits=[("act", a_s)])
        M["s1_done_pe"] = P.cnt["pe"]
        M["s1_done_dve"] = P.cnt["dve"]
        _dump("ha", sb["a_ha"][:, :, :], [128, 16, TOK], BF16,
              [("dve", M["s1_done_dve"])])
        _dump("hb", sb["c_hb"][:, :, :], [128, 16, TOK], BF16,
              [("dve", M["s1_done_dve"])])

        def em_hstage(tag, wsrc, wgate, scale_t, outdram, first_waits,
                      split_tb=False):
            for hc2 in range(32):
                pgw = hc2 % 2
                def f_w(e, hc2=hc2, pgw=pgw, wsrc=wsrc):
                    return e.dma_start(sb["wst"][:, pgw, :, :], wsrc[hc2])
                w_ld = [("pe", M.get((tag, "mm", 4 * (hc2 - 2) + 3), 0)),
                        wgate]
                if hc2 < 2:
                    w_ld += [("pe", M[(tag, "prev_pe")])]
                sp_w = P.emit("sp", f_w, waits=w_ld,
                              stream=f"swst{hc2 % 2}")
                for tb in range(4):
                    fl = 4 * hc2 + tb
                    bk_ = fl % 8
                    pst = psA if bk_ < 4 else psB
                    def f_mm(t_, tb=tb, pgw=pgw, pst=pst, bk_=bk_):
                        r = None
                        for dc in range(32):
                            hv = sb["a_ha"] if dc < 16 else sb["c_hb"]
                            r = t_.matmul(bank(pst, bk_ % 4),
                                          sb["wst"][:, pgw, dc, :],
                                          hv[:, dc % 16,
                                             512 * tb:512 * (tb + 1)],
                                          start=(dc == 0), stop=(dc == 31))
                        return r
                    w_mm = [(f"swst{hc2 % 2}", sp_w),
                            ("act", M.get((tag, "act", fl - 8), 0))]
                    if fl == 0:
                        w_mm += first_waits
                    pe_m = P.emit("pe", f_mm, waits=w_mm)
                    M[(tag, "mm", fl)] = pe_m
                    pga = hc2 % 2
                    def f_gelu(s, hc2=hc2, tb=tb, pga=pga, pst=pst, bk_=bk_):
                        return s.activation(
                            sb["c_hp"][:, pga, 512 * tb:512 * (tb + 1)],
                            bank(pst, bk_ % 4), AF.Gelu,
                            scale=scale_t[:, hc2:hc2 + 1])
                    a_g = P.emit("act", f_gelu,
                                 waits=[("pe", pe_m),
                                        (f"sspg{hc2 % 2}",
                                         M.get((tag, "spill", hc2 - 2), 0))])
                    M[(tag, "act", fl)] = a_g
                if split_tb:
                    last = 0
                    for tb2 in range(4):
                        def f_out(e, hc2=hc2, pga=pga, tb2=tb2):
                            return e.dma_start(
                                outdram[tb2, :, hc2, :],
                                sb["c_hp"][:, pga, 512 * tb2:512 * (tb2 + 1)])
                        last = P.emit("gp", f_out, stream=f"sspg{hc2 % 2}",
                                      waits=[("act", a_g)])
                    M[(tag, "spill", hc2)] = last
                else:
                    def f_out(e, hc2=hc2, pga=pga):
                        return e.dma_start(outdram[hc2, :, :],
                                           sb["c_hp"][:, pga, :])
                    M[(tag, "spill", hc2)] = P.emit(
                        "gp", f_out, stream=f"sspg{hc2 % 2}",
                        waits=[("act", a_g)])

        M[("h1", "prev_pe")] = M["s1_done_pe"]
        em_hstage("h1", dr["h1t_bf"], ("gh1", M["h1t_bf"]), sb["sh1"],
                  dr["hp"], [("dve", M["s1_done_dve"])])
        _dump("hp", dr["hp"].ap(), [32, 128, TOK], BF16,
              [("sspg0", M[("h1", "spill", 30)]),
               ("sspg1", M[("h1", "spill", 31)])])

        for hc in range(32):
            hv = sb["a_ha"] if hc < 16 else sb["c_hb"]
            def f_ld(e, hc=hc, hv=hv):
                return e.dma_start(hv[:, hc % 16, :], dr["hp"][hc])
            eng_h = "sp" if hc % 2 == 0 else "gp"
            M[("hpload", hc)] = P.emit(
                eng_h, f_ld, stream="shp" if hc % 2 == 0 else "shpg",
                waits=[("pe", M[("h1", "mm", 127)]),
                       (f"sspg{hc % 2}", M[("h1", "spill", hc)])])

        M[("h2", "prev_pe")] = M[("h1", "mm", 127)]
        em_hstage("h2", dr["h2t_bf"], ("gh2", M["h2t_bf"]), sb["sh2"],
                  dr["hpp"], [("shp", M[("hpload", 30)]),
                              ("shpg", M[("hpload", 31)])], split_tb=True)

        pyb = [sb["p_y"], sb["p_y2"]]
        for oc3 in range(3):
            def f_wr(e, oc3=oc3):
                return e.dma_start(sb["w3r"][:, oc3, :, :],
                                   dr["w3t_bf"][oc3])
            M["w3r_ld"] = P.emit(
                "sp", f_wr, stream="sw3r",
                waits=[("pe", M["s1_done_pe"]), ("gw3", M["w3t_bf"])])
        for pr in range(2):
            sp_hs = {}
            for t_ in range(2):
                tb = 2 * pr + t_
                def f_hpp(e, tb=tb, t_=t_):
                    return e.dma_start(sb["a_hppt"][:, t_, :, :],
                                       dr["hpp"][tb])
                w = [("sspg0", M[("h2", "spill", 30)]),
                     ("sspg1", M[("h2", "spill", 31)]),
                     ("pe", M.get(("w3pair_mm", pr - 1), 0))]
                sp_hs[t_] = P.emit("sp", f_hpp, waits=w,
                                   stream=f"shp{t_}")
            pe_m = 0
            for oc in range(8):
                pgw = oc % 2
                if oc < 3:
                    wslc = sb["w3r"][:, oc, :, :]
                    wwait = ("sw3r", M["w3r_ld"])
                else:
                    def f_w(e, oc=oc, pgw=pgw):
                        return e.dma_start(sb["wst"][:, pgw, :, :],
                                           dr["w3t_bf"][oc])
                    w_ld = [("pe",
                             M.get(("w3mmf", 16 * pr + 2 * (oc - 2) + 1),
                                   0)),
                            ("gw3", M["w3t_bf"])]
                    if pr == 0 and oc < 5:
                        w_ld += [("pe", M[("h2", "mm", 127)])]
                    sp_w = P.emit("sp", f_w, waits=w_ld,
                                  stream=f"swst{oc % 2}")
                    wslc = sb["wst"][:, pgw, :, :]
                    wwait = (f"swst{oc % 2}", sp_w)
                for t_ in range(2):
                    tb = 2 * pr + t_
                    flw = 16 * pr + 2 * oc + t_
                    bb_ = flw % 4
                    def f_mm(t2_, wslc=wslc, t_=t_, bb_=bb_):
                        r = None
                        for dc in range(32):
                            r = t2_.matmul(bank(psA, bb_),
                                           wslc[:, dc, :],
                                           sb["a_hppt"][:, t_, dc, :],
                                           start=(dc == 0), stop=(dc == 31))
                        return r
                    w_mm = [wwait, (f"shp{t_}", sp_hs[t_]),
                            ("dve", M.get(("w3evf", flw - 4), 0))]
                    pe_m = P.emit("pe", f_mm, waits=w_mm)
                    M[("w3mmf", flw)] = pe_m
                    def f_ev(vv, oc=oc, t_=t_, bb_=bb_):
                        return vv.tensor_scalar(pyb[t_][:, oc, :],
                                                bank(psA, bb_),
                                                sb["s3"][:, oc:oc + 1], None,
                                                ALU.mult)
                    M[("w3evf", flw)] = P.emit(
                        "dve", f_ev,
                        waits=[("pe", pe_m),
                               ("pe", M.get((("l4", tb - 2), "sx_pe"), 0)),
                               (f"sout{t_}", M.get(("outdma", tb - 2), 0))])
                    M[("w3ev", tb, oc)] = M[("w3evf", flw)]
            M[("w3pair_mm", pr)] = pe_m
            for t_ in range(2):
                tb = 2 * pr + t_
                def f_xo(e, tb=tb):
                    return e.dma_start(sb["p_xot"][:, :, :],
                                       dr["x_own_tb"][tb])
                w_r = [("dve", M.get(("w3res", tb - 1), 0))]
                sp_xo = P.emit("sp", f_xo, waits=w_r, stream="sxo")
                def f_t2l(e, tb=tb):
                    return e.dma_start(sb["p_t2"][:, :, :], dr["t2sp"][tb])
                sp_t2 = P.emit("sp", f_t2l, stream="sxo")
                ya = [pyb[t_][:, c, :] for c in range(8)]
                tw = [("dve", M[("w3ev", tb, 7)])]
                sd4, ld4, apg4, ci4 = em_stats_chain(
                    ya, 512, ("l4", tb), sb["p_sq"], extra_pe=tw,
                    extra_dve=tw, abuf=sb["absc2"])
                d_last = None
                for c in range(8):
                    def f_t(vv, c=c, t_=t_, apg4=apg4):
                        return vv.tensor_mul(sb["p_sq2"][:, c % 2, :],
                                             pyb[t_][:, c, :],
                                             sb["absc2"][:, apg4, 512:1024])
                    P.emit("dve", f_t, waits=[(sd4, ld4)])
                    def f_t4(vv, c=c, t_=t_, apg4=apg4):
                        return vv.tensor_add(pyb[t_][:, c, :],
                                             sb["p_sq2"][:, c % 2, :],
                                             sb["absc2"][:, apg4, 0:512])
                    P.emit("dve", f_t4)
                    def f_o1(vv, c=c):
                        return vv.tensor_add(sb["p_sq2"][:, c % 2, :],
                                             sb["p_xot"][:, c, :],
                                             sb["p_t2"][:, c, :])
                    P.emit("dve", f_o1, waits=[("sxo", sp_t2),
                                               ("sxo", sp_xo)])
                    def f_o2(vv, c=c, t_=t_):
                        return vv.tensor_add(pyb[t_][:, c, :],
                                             pyb[t_][:, c, :],
                                             sb["p_sq2"][:, c % 2, :])
                    d_last = P.emit("dve", f_o2)
                M[("w3res", tb)] = d_last
                M[("abs_use", ci4)] = d_last
                def f_out(e, tb=tb, t_=t_):
                    return e.dma_start(dr["out"][tb], pyb[t_][:, :, :])
                M[("outdma", tb)] = P.emit("gp", f_out, stream=f"sout{t_}",
                                           waits=[("dve", d_last)])
        P.wait_only("sp", [("sout0", M[("outdma", 2)]),
                           ("sout1", M[("outdma", 3)])])

    em_prologue()
    def _pre(i):
        if 8 <= i + 3 <= 39:
            em_xload(i + 3)
        em_chain(i)
    for _i in range(8):
        if _i == 6:
            em_qab(0)
            em_qab(1)
        _pre(_i)
    for _t in range(4):
        em_qapply(_t, 0)
        _pre(8 * _t + 8)
        em_qapply(_t, 1)
        _pre(8 * _t + 9)
        if _t < 3:
            em_qab(2 * _t + 2)
            em_qab(2 * _t + 3)
        _pre(8 * _t + 10)
        _pre(8 * _t + 11)
        em_qproj(_t)
        for _i in range(8 * _t + 12, 8 * _t + 16):
            _pre(_i)
    em_qfinish()
    _casts = mk_cast_list()
    _dump("q", sb["b_q"][:, :, :], [128, 8, TOK], BF16,
          [("act", M["q_done_act"])])
    for g in range(NG):
        if g == 0:
            em_group_ln(0)
        em_group_kv(g)
        if g == 0:
            _dump("xln0", sb["xln"][:, :, :], [128, 8, 512], BF16,
                  [("gga", M[(0, "apply_last")])])
            _dump("kg0", sb["kg"][:, :, :], [128, 8, 512], BF16,
                  [("act", M[(0, "kv_act_last")])])
            _dump("vst0", sb["vst"][:, :, :, :], [128, 4, 16, 128], BF16,
                  [("dve", M[(0, "vcopy_last")])])
        em_visits(g)
        M[(g, "vend_pe")] = P.cnt["pe"]
        em_casts(_casts, 9, [("pe", M[(g, "kv_last")])])
    _lastv = vstate["v"] - 1
    _dump("ctx", sb["a_ctx"][:, :, :], [128, 8, TOK], F32,
          [("dve", D_hist[_lastv]), ("act", E_hist[_lastv])])
    _dump("den", sb["den"][:, :, :], [128, 4, 512], F32,
          [("dve", D_hist[_lastv]), ("act", E_hist[_lastv])])
    em_postattn()
    _dump("x1", sb["a_x1"][:, :, :], [128, 8, TOK], F32,
          [("dve", M[("pa_xo", 3)])])
    _dump("xln2", sb["b_xln2"][:, :, :], [128, 8, TOK], BF16,
          [("dve", M["xln2_done"])])
    _dump("t2", dr["t2sp"].ap(), [4, 128, 8, 512], F32,
          [("st2s", M[("t2spill", 3)])])
    _dump("ab", dr["ab"].ap(), [40, 512], F32,
          [("sabw0", M[("pp_ab", 36)]), ("sabw1", M[("pp_ab", 37)]),
           ("sabw2", M[("pp_ab", 38)]), ("sabw3", M[("pp_ab", 39)])])
    em_ffn()

    with contextlib.ExitStack() as stack:
        sem_objs = {s: stack.enter_context(nc.semaphore(f"sem_{s}"))
                    for s in Prog.STREAMS if P.cnt[s] > 0 or s in Prog.ENG}
        blk = stack.enter_context(nc.Block())

        def replay(ename):
            def run(eng):
                waited = {s: 0 for s in Prog.STREAMS}
                for waits, fn, inc, strm in P.ops[ename]:
                    for (s2, v) in waits:
                        if waited[s2] < v:
                            eng.wait_ge(sem_objs[s2], v)
                            waited[s2] = v
                    if fn is None:
                        continue
                    inst = fn(eng)
                    if inc:
                        inst.then_inc(sem_objs[strm], inc)
            return run

        blk.tensor(replay("pe"))
        blk.scalar(replay("act"))
        blk.vector(replay("dve"))
        blk.sync(replay("sp"))
        blk.gpsimd(replay("gp"))

    return nc


_NC_CACHE = [None]


def kernel(**inputs):
    per_core = build_host_inputs(inputs)
    if _NC_CACHE[0] is None:
        _NC_CACHE[0] = build_nc()
    nc = _NC_CACHE[0]
    res = run_bass_kernel_spmd(nc, per_core, core_ids=list(range(NCORE)))
    x = np.asarray(inputs["L0_attn_states"], np.float32)
    out = np.empty_like(x)
    for c in range(NCORE):
        b, i = c // 4, c % 4
        oc = np.asarray(res.results[c]["out"])
        for k in range(4):
            s = 4 * k + i
            out[b, 512 * s:512 * (s + 1), :] = (
                oc[k].transpose(2, 1, 0).reshape(512, 1024))
    return out


# version 43 (final)
# speedup vs baseline: 1.1965x; 1.1965x over previous
import contextlib
import numpy as np
import ml_dtypes
import concourse.bass as bass
import concourse.mybir as mybir
from concourse.bass_utils import run_bass_kernel_spmd

F32 = mybir.dt.float32
BF16 = mybir.dt.bfloat16
AF = mybir.ActivationFunctionType
ALU = mybir.AluOpType

NCORE = 8
B, S, D = 2, 8192, 1024
DH = 4096
NG = 16
GSZ = 512
TOK = 2048

SBUF_BASE = 16512
OFF_A = SBUF_BASE + 0
OFF_B = SBUF_BASE + 65536
OFF_C = SBUF_BASE + 98304
OFF_D = SBUF_BASE + 172032
C_DEN = OFF_C + 0
C_XG = OFF_C + 8192
C_SQ = OFF_C + 16384
C_TL = OFF_C + 18432
C_KG = OFF_C + 20480
C_VST = OFF_C + 28672
C_P = OFF_C + 45056
C_XLN = OFF_C + 49152
C_RSEL = OFF_C + 57344
C_WK = OFF_C + 61440
C_FREE = OFF_C + 65536
D_ROWS = OFF_D + 0
D_CPOOL = OFF_D + 4096
D_BIAS = OFF_D + 4160
D_SCAL = OFF_D + 4288
D_ONESCOL = OFF_D + 4800
D_ONESROW = OFF_D + 4832
D_ONESROW_BF = OFF_D + 5344
D_BVROW = OFF_D + 5600
D_L = OFF_D + 7648
D_WST = OFF_D + 7904
D_WV = OFF_D + 24288
D_END = OFF_D + 40672
assert D_END <= 229344


def _bf(a):
    return np.ascontiguousarray(a).astype(ml_dtypes.bfloat16)


def build_host_inputs(inputs):
    x = np.asarray(inputs["L0_attn_states"], np.float32)
    def pcol(w):
        w = np.asarray(w, np.float32)
        return np.ascontiguousarray(w.reshape(8, 128, w.shape[1])
                                    .transpose(1, 0, 2))

    def tile4(wt, nco):
        wt = np.asarray(wt, np.float32)
        di = wt.shape[0] // 128
        return np.ascontiguousarray(
            wt.reshape(di, 128, nco, 128).transpose(2, 1, 0, 3))

    w1tl = tile4(np.asarray(inputs["L0_ffn_w1_w"], np.float32).T, 32)
    w2tl = tile4(np.asarray(inputs["L0_ffn_w2_w"], np.float32).T, 32)
    shared = {
        "wq": pcol(inputs["L0_attn_Wq"]),
        "wk": tile4(inputs["L0_attn_Wk"], 8),
        "wv": pcol(inputs["L0_attn_Wv"]),
        "wo": pcol(inputs["L0_attn_Wo"]),
        "w12t": np.ascontiguousarray(np.concatenate([w1tl, w2tl], axis=2)),
        "h1t": tile4(np.asarray(inputs["L0_ffn_h1_w"], np.float32).T, 32),
        "h2t": tile4(np.asarray(inputs["L0_ffn_h2_w"], np.float32).T, 32),
        "w3t": tile4(np.asarray(inputs["L0_ffn_w3_w"], np.float32).T, 8),
    }

    def vec8(v):
        return np.ascontiguousarray(np.asarray(v, np.float32).reshape(8, 128).T)

    def vec32(v):
        return np.ascontiguousarray(np.asarray(v, np.float32).reshape(32, 128).T)

    shared["bq_sb"] = vec8(inputs["L0_attn_bq"])
    shared["bk_sb"] = vec8(inputs["L0_attn_bk"])
    shared["bo_sb"] = vec8(inputs["L0_attn_bo"])
    shared["s3_sb"] = vec8(inputs["L0_ffn_w3_s"])
    shared["s1_sb"] = vec32(inputs["L0_ffn_w1_s"])
    shared["s2_sb"] = vec32(inputs["L0_ffn_w2_s"])
    shared["sh1_sb"] = vec32(inputs["L0_ffn_h1_s"])
    shared["sh2_sb"] = vec32(inputs["L0_ffn_h2_s"])
    shared["bv_row"] = np.ascontiguousarray(
        np.asarray(inputs["L0_attn_bv"], np.float32).reshape(1, D))
    kk = np.arange(128)
    shared["Lmat"] = _bf(np.where(kk[None, :] >= kk[:, None],
                                  np.float32(-1600.0), np.float32(0.0)))
    SEL = np.zeros((8, 128, 128), np.float32)
    for c in range(8):
        for half in range(2):
            h = 2 * c + half
            denrow = h if (h % 2 == 1) else 64 + h
            SEL[c, denrow, half * 64:(half + 1) * 64] = 1.0
    shared["SEL"] = np.ascontiguousarray(SEL.transpose(1, 0, 2))

    xT = [np.ascontiguousarray(x[b].T) for b in range(B)]
    xtl = [np.ascontiguousarray(t.reshape(8, 128, 32, 256)
                                .transpose(2, 1, 0, 3)) for t in xT]
    per_core = []
    for c in range(NCORE):
        b, i = c // 4, c % 4
        x_own = np.ascontiguousarray(np.concatenate(
            [xT[b][:, 512 * (4 * k + i):512 * (4 * k + i) + 512]
             for k in range(4)], axis=1))
        R = np.zeros((NG, 4, 128, 512), np.float32)
        for g in range(NG):
            s_k0 = 4 * (g // 4) + i
            if s_k0 > g:
                continue
            for kc in range(4):
                if s_k0 < g:
                    R[g, kc, 0, :] = 1.0
                else:
                    q = np.arange(512)
                    f = q - 128 * kc + 1
                    R[g, kc, 0, f <= 0] = 1.0
                    sel = (f >= 1) & (f <= 127)
                    R[g, kc, f[sel], q[sel]] = 1.0
        x_own_t = np.ascontiguousarray(
            x_own.reshape(8, 128, 2048).transpose(1, 0, 2))
        x_own_tb = np.ascontiguousarray(
            x_own_t.reshape(128, 8, 4, 512).transpose(2, 0, 1, 3))
        m = {"x_tiled": xtl[b], "x_own": x_own_t, "x_own_tb": x_own_tb,
             "rsel": _bf(R.transpose(0, 2, 1, 3))}
        m.update(shared)
        per_core.append(m)
    return per_core


class Prog:
    ENG = ("pe", "act", "dve", "sp", "gp")
    STREAMS = ("pe", "act", "dve", "sp", "gp", "sv", "sxo", "srs", "sx",
               "sab", "sab0", "sab1", "sabw0", "sabw1", "sabw2", "sabw3",
               "gpa", "gga", "gpp", "sabc0", "sabc1", "sabd0", "sabd1",
               "sx0", "sx1", "sx2", "sx3", "sgx0", "sgx1", "sag0", "sag1",
               "swk", "swk0", "swk1",
               "swst", "swst0", "swst1", "ssp", "sspg0", "sspg1", "st2s",
               "sout", "sout0", "sout1", "sw3r", "shp", "shpg", "shp0", "shp1",
               "sdump",
               "gq", "gwk", "gwv", "gbv", "gw1", "gw2", "gh1", "gh2", "gw3",
               "gwo")

    def __init__(self):
        self.ops = {e: [] for e in self.ENG}
        self.cnt = {s: 0 for s in self.STREAMS}

    def emit(self, eng, fn, waits=(), inc=None, stream=None):
        s = stream or eng
        if inc is None:
            inc = 16 if eng in ("sp", "gp") else 1
        self.ops[eng].append(([(e, v) for (e, v) in waits if v > 0], fn,
                              inc, s))
        self.cnt[s] += inc
        return self.cnt[s]

    def wait_only(self, eng, waits):
        self.ops[eng].append(([(e, v) for (e, v) in waits if v > 0], None,
                              0, eng))


CP = {"scale8": 0, "mexp": 1, "eps": 2, "mhalf": 3, "inv1024": 4,
      "mone": 5, "tiny": 6, "one": 7}
CPV = {"scale8": 0.125, "mexp": -8.0, "eps": 1e-5, "mhalf": -0.5,
       "inv1024": 1.0 / 1024.0, "mone": -1.0, "tiny": 1e-30, "one": 1.0}

_DEBUG_DUMPS = []
_PIPE_AV = True


def build_nc():
    nc = bass.Bass("TRN2", target_bir_lowering=False)

    dr = {}
    dr["x_tiled"] = nc.dram_tensor("x_tiled", [32, 128, 8, 256], F32,
                                   kind="ExternalInput")
    dr["x_own"] = nc.dram_tensor("x_own", [128, 8, TOK], F32,
                                 kind="ExternalInput")
    dr["x_own_tb"] = nc.dram_tensor("x_own_tb", [4, 128, 8, 512], F32,
                                    kind="ExternalInput")
    dr["rsel"] = nc.dram_tensor("rsel", [NG, 128, 4, 512], BF16,
                                kind="ExternalInput")
    for n, shp in [("wq", [128, 8, D]), ("wk", [8, 128, 8, 128]),
                   ("wv", [128, 8, D]), ("wo", [128, 8, D]),
                   ("w12t", [32, 128, 16, 128]), ("h1t", [32, 128, 32, 128]),
                   ("h2t", [32, 128, 32, 128]), ("w3t", [8, 128, 32, 128])]:
        dr[n] = nc.dram_tensor(n, shp, F32, kind="ExternalInput")
    for n in ["bq_sb", "bk_sb", "bo_sb", "s3_sb"]:
        dr[n] = nc.dram_tensor(n, [128, 8], F32, kind="ExternalInput")
    for n in ["s1_sb", "s2_sb", "sh1_sb", "sh2_sb"]:
        dr[n] = nc.dram_tensor(n, [128, 32], F32, kind="ExternalInput")
    dr["bv_row"] = nc.dram_tensor("bv_row", [1, D], F32, kind="ExternalInput")
    dr["Lmat"] = nc.dram_tensor("Lmat", [128, 128], BF16, kind="ExternalInput")
    dr["SEL"] = nc.dram_tensor("SEL", [128, 8, 128], F32,
                               kind="ExternalInput")
    dr["out"] = nc.dram_tensor("out", [4, 128, 8, 512], F32,
                               kind="ExternalOutput")
    dr["wk_bf"] = nc.dram_tensor("wk_bf", [8, 128, 8, 128], BF16)
    dr["w12t_bf"] = nc.dram_tensor("w12t_bf", [32, 128, 16, 128], BF16)
    dr["h1t_bf"] = nc.dram_tensor("h1t_bf", [32, 128, 32, 128], BF16)
    dr["h2t_bf"] = nc.dram_tensor("h2t_bf", [32, 128, 32, 128], BF16)
    dr["w3t_bf"] = nc.dram_tensor("w3t_bf", [8, 128, 32, 128], BF16)
    dr["t2sp"] = nc.dram_tensor("t2sp", [4, 128, 8, 512], F32)
    dr["ab"] = nc.dram_tensor("ab", [40, 512], F32)
    dr["ab2"] = nc.dram_tensor("ab2", [2, 1024], F32)
    dr["hp"] = nc.dram_tensor("hp", [32, 128, TOK], BF16)
    dr["hpp"] = nc.dram_tensor("hpp", [4, 128, 32, 512], BF16)

    sb = {}

    def T(name, shape, dtype, off):
        sb[name] = nc.alloc_sbuf_tensor_at(name, shape, dtype, offset=off)

    T("a_xown", [128, 8, TOK], F32, OFF_A)
    T("a_ctx", [128, 8, TOK], F32, OFF_A)
    T("a_x1", [128, 8, TOK], F32, OFF_A)
    T("a_ha", [128, 16, TOK], BF16, OFF_A)
    T("a_hppt", [128, 2, 32, 512], BF16, OFF_A)
    T("b_q", [128, 8, TOK], BF16, OFF_B)
    T("b_ctxfm", [128, 8, TOK], BF16, OFF_B)
    T("b_xln2", [128, 8, TOK], BF16, OFF_B)
    T("den", [128, 4, 512], F32, C_DEN)
    T("xg", [128, 8, 256], F32, C_XG)
    T("sq", [128, 8, 256], F32, C_SQ)
    T("tl", [128, 512], F32, C_TL)
    T("kg", [128, 8, 512], BF16, C_KG)
    T("vst", [128, 4, 16, 128], BF16, C_VST)
    T("pbuf", [128, 4, 512], BF16, C_P)
    T("pbuf2", [128, 4, 512], BF16, C_FREE)
    T("xln", [128, 8, 512], BF16, C_XLN)
    T("rselb", [128, 4, 512], BF16, C_RSEL)
    T("wkb", [128, 2, 8, 128], BF16, C_WK)
    T("wq_sb", [128, 8, 1024], BF16, D_WST)
    T("wo_sb", [128, 8, 1024], BF16, C_KG)
    T("sel_sb", [128, 8, 128], F32, OFF_C + 36864)
    T("lnd", [128, 4, 512], F32, C_XG)
    T("p_y", [128, 8, 512], F32, OFF_C + 0)
    T("p_xot", [128, 8, 512], F32, OFF_C + 40960)
    T("p_t2", [128, 8, 512], F32, OFF_C + 57344)
    T("p_sq", [128, 2, 512], F32, OFF_C + 16384)
    T("p_y2", [128, 8, 512], F32, OFF_C + 20480)
    T("c_hb", [128, 16, TOK], BF16, OFF_C)
    T("c_hp", [128, 2, TOK], BF16, C_FREE)
    T("c_g1", [128, 2, 512], BF16, C_FREE)
    T("rows", [1, 1024], F32, D_ROWS)
    T("rows_pp", [1, 2048], F32, C_VST)
    for _t in range(4):
        T(f"tlq{_t}", [128, 256], F32, OFF_B + 1024 * _t)
    T("gxb", [128, 8, 256], F32, D_WST)
    T("rselb2", [128, 4, 512], BF16, D_WST + 8192)
    T("absg", [128, 2, 512], F32, D_WST + 12288)
    T("absq", [128, 2, 512], F32, C_P)
    T("absc", [128, 2, 1024], F32, D_WST)
    T("absc2", [128, 2, 1024], F32, D_WV + 8192)
    T("p_sq2", [128, 2, 512], F32, OFF_C + 36864)
    T("rows2", [1, 1024], F32, D_WV)
    T("p_sq3", [128, 2, 512], F32, D_WV + 4096)
    T("vstg", [128, 2, 512], BF16, C_SQ)
    T("w3r", [128, 4, 32, 128], BF16, OFF_B)
    T("xg4", [128, 8, 256], F32, C_VST + 8192)
    T("xg3", [128, 8, 256], F32, C_FREE)
    T("cpool", [128, 16], F32, D_CPOOL)
    T("bq", [128, 8], F32, D_BIAS + 0)
    T("bk", [128, 8], F32, D_BIAS + 32)
    T("bo", [128, 8], F32, D_BIAS + 64)
    T("s3", [128, 8], F32, D_BIAS + 96)
    T("s1", [128, 32], F32, D_SCAL + 0)
    T("s2", [128, 32], F32, D_SCAL + 128)
    T("sh1", [128, 32], F32, D_SCAL + 256)
    T("sh2", [128, 32], F32, D_SCAL + 384)
    T("onescol", [128, 1], F32, D_ONESCOL)
    T("onesrow", [1, 128], F32, D_ONESROW)
    T("onesrow_bf", [1, 128], BF16, D_ONESROW_BF)
    T("bv_bf", [1, 1024], BF16, D_BVROW)
    T("lmat", [128, 128], BF16, D_L)
    T("wst", [128, 2, 32, 128], BF16, D_WST)
    T("wv_sb", [128, 8, 1024], BF16, D_WV)

    psA = nc.alloc_psum_tensor("psA", [128, 2048], F32)
    psB = nc.alloc_psum_tensor("psB", [128, 2048], F32)
    ps = [psA, psB]

    P = Prog()
    M = {}

    def bank(t, j):
        return t[:, 512 * j:512 * (j + 1)]

    def _dump(name, ap, shape, dtype, waits):
        if name not in _DEBUG_DUMPS:
            return
        d = nc.dram_tensor("dbg_" + name, list(shape), dtype,
                           kind="ExternalOutput")
        def f(e, d=d, ap=ap):
            return e.dma_start(d.ap(), ap)
        v = P.emit("sp", f, waits=waits, stream="sdump")
        P.wait_only("sp", [("sdump", v)])

    def cp(name, p0=0, pn=128):
        return sb["cpool"][p0:p0 + pn, CP[name]:CP[name] + 1]

    rows = sb["rows"]

    def em_prologue():
        def f_memsets(v):
            for n, val in CPV.items():
                v.memset(sb["cpool"][:, CP[n]:CP[n] + 1], val)
            v.memset(sb["onescol"][:, :], 1.0)
            v.memset(sb["onesrow"][:, :], 1.0)
            return v.memset(sb["onesrow_bf"][:, :], 1.0)
        M["memsets"] = P.emit("dve", f_memsets)

        def dma(dst_ap, src_ap):
            def f(e, d=dst_ap, s=src_ap):
                return e.dma_start(d, s)
            return f
        for n, t in [("bq_sb", "bq"), ("bk_sb", "bk"), ("bo_sb", "bo"),
                     ("s3_sb", "s3"), ("s1_sb", "s1"), ("s2_sb", "s2"),
                     ("sh1_sb", "sh1"), ("sh2_sb", "sh2")]:
            P.emit("sp", dma(sb[t][:, :], dr[n][:, :]), stream="sv")
        P.emit("sp", dma(sb["lmat"][:, :], dr["Lmat"][:, :]), stream="sv")
        M["xown"] = P.emit("sp", dma(sb["a_xown"][:, :, :],
                                     dr["x_own"].ap()), stream="sxo")
        M[(0, "rsel")] = P.emit("sp", dma(sb["rselb"][:, :, :],
                                          dr["rsel"][0]), stream="srs")
        M["vecs"] = P.cnt["sv"]

        M["wq"] = P.emit("gp", dma(sb["wq_sb"][:, :, :], dr["wq"].ap()),
                         stream="gq")
        for cc in range(8):
            P.emit("gp", dma(dr["wk_bf"][cc], dr["wk"][cc]), stream="gwk")
        M["wk_bf"] = P.cnt["gwk"]
        M["wv"] = P.emit("gp", dma(sb["wv_sb"][:, :, :], dr["wv"].ap()),
                         stream="gwv")
        M["bv"] = P.emit("gp", dma(sb["bv_bf"][:, :], dr["bv_row"][:, :]),
                         stream="gbv")
    def mk_cast_list():
        def dma(dst_ap, src_ap):
            def f(e, d=dst_ap, s=src_ap):
                return e.dma_start(d, s)
            return f
        casts = []
        for hc in range(32):
            casts.append(("gw1", dma(dr["w12t_bf"][hc], dr["w12t"][hc])))
        for hc in range(32):
            casts.append(("gh1", dma(dr["h1t_bf"][hc], dr["h1t"][hc])))
        for hc in range(32):
            casts.append(("gh2", dma(dr["h2t_bf"][hc], dr["h2t"][hc])))
        for oc in range(8):
            casts.append(("gw3", dma(dr["w3t_bf"][oc], dr["w3t"][oc])))
        return casts

    def em_casts(casts, n, gate):
        first = True
        while casts and n > 0:
            strm, f = casts.pop(0)
            P.emit("gp", f, stream=strm, waits=gate if first else ())
            first = False
            n -= 1
        if not casts:
            M["w12t_bf"] = P.cnt["gw1"]
            M["h1t_bf"] = P.cnt["gh1"]
            M["h2t_bf"] = P.cnt["gh2"]
            M["w3t_bf"] = P.cnt["gw3"]

    def em_stats_chain(xa, ntok, tag, sq_t, extra_pe=(), extra_dve=(),
                       extra_ld=(), abuf=None):
        ci0 = M.get("abc_i", -1) + 1
        rbuf = sb["rows"] if ci0 % 2 == 0 else sb["rows2"]
        sqb = sq_t if ci0 % 2 == 0 else sb["p_sq3"]
        base = 1024 if ci0 % 2 == 0 else 0
        """xa: 8 APs [128, ntok] f32.  sq_t: [128, 2, ntok] f32 ping buffer.
        Produces a-row at rows[512:512+ntok], b-row at rows[0:ntok], and
        broadcasts a_b -> psB[:, 0:ntok], b_b -> psB[:, 512:512+ntok].
        Returns pe value after broadcasts."""
        def f_sx(t):
            r = None
            for c in range(8):
                r = t.matmul(psB[0:1, base:base + ntok], sb["onescol"][:, :],
                             xa[c], start=(c == 0), stop=(c == 7))
            return r
        pe_sx = P.emit("pe", f_sx, waits=list(extra_pe))
        M[(tag, "sx_pe")] = pe_sx
        last_pe = None
        for c in range(8):
            pg = c % 2
            if c < 2:
                wd = [("pe", M.get(("sx2t", ci0 % 2), 0))] + list(extra_dve)
            else:
                wd = [("pe", M.get((tag, "sx2", c - 2), 0))] + list(extra_dve)
            def f_sqc(v, c=c, pg=pg):
                return v.tensor_mul(sqb[:, pg, :], xa[c], xa[c])
            d_sq = P.emit("dve", f_sqc, waits=wd)
            def f_sx2(t, c=c, pg=pg):
                return t.matmul(psB[0:1, base + 512:base + 512 + ntok],
                                sb["onescol"][:, :], sqb[:, pg, :],
                                start=(c == 0), stop=(c == 7))
            last_pe = P.emit("pe", f_sx2, waits=[("dve", d_sq)])
            M[(tag, "sx2", c)] = last_pe
        M[("sx2t", ci0 % 2)] = last_pe

        def f_mu(v):
            return v.tensor_scalar(rbuf[0:1, 0:ntok],
                                   psB[0:1, base:base + ntok],
                                   cp("inv1024", 0, 1), None, ALU.mult)
        P.emit("dve", f_mu,
               waits=[("pe", pe_sx),
                      (f"sabc{ci0 % 2}", M.get(f"abcw{ci0 % 2}", 0))])

        def f_mu2(v):
            return v.tensor_mul(rbuf[0:1, 512:512 + ntok], rbuf[0:1, 0:ntok],
                                rbuf[0:1, 0:ntok])
        P.emit("dve", f_mu2)

        def f_var(v):
            return v.scalar_tensor_tensor(
                rbuf[0:1, 512:512 + ntok], psB[0:1, base + 512:base + 512 + ntok],
                cp("inv1024", 0, 1), rbuf[0:1, 512:512 + ntok],
                ALU.mult, ALU.subtract)
        d_var = P.emit("dve", f_var, waits=[("pe", last_pe)])

        def f_ln(s):
            return s.activation(rbuf[0:1, 512:512 + ntok],
                                rbuf[0:1, 512:512 + ntok], AF.Ln,
                                bias=cp("eps", 0, 1))
        P.emit("act", f_ln, waits=[("dve", d_var)])

        def f_exp(s):
            return s.activation(rbuf[0:1, 512:512 + ntok],
                                rbuf[0:1, 512:512 + ntok], AF.Exp,
                                scale=cp("mhalf", 0, 1))
        a_a = P.emit("act", f_exp)

        def f_b(v):
            return v.scalar_tensor_tensor(rbuf[0:1, 0:ntok], rbuf[0:1, 0:ntok],
                                          cp("mone", 0, 1),
                                          rbuf[0:1, 512:512 + ntok],
                                          ALU.mult, ALU.mult)
        d_b = P.emit("dve", f_b, waits=[("act", a_a)])

        ci = M["abc_i"] = M.get("abc_i", -1) + 1
        pg2 = ci % 2
        def f_w(e, pg2=pg2):
            return e.dma_start(dr["ab2"][pg2:pg2 + 1, :], rbuf[0:1, 0:1024])
        wcnt = P.emit("sp", f_w, waits=[("dve", d_b)], stream=f"sabc{pg2}")
        M[f"abcw{pg2}"] = wcnt
        if abuf is None:
            abuf = sb["absc"]
        def f_l(e, pg2=pg2, abuf=abuf):
            return e.dma_start(
                abuf[:, pg2, :],
                dr["ab2"].ap().partition_broadcast(128)[:, pg2, :])
        wl = [(f"sabc{pg2}", wcnt),
              ("dve", M.get(("abs_use", ci - 2), 0))] + list(extra_ld)
        ld = P.emit("sp", f_l, waits=wl, stream=f"sabd{pg2}")
        M[(tag, "abs")] = (pg2, ld, ci)
        if tag == ("pa", 0):
            _dump("absc0", sb["absc"][:, pg2, :], [128, 1024], F32,
                  [(f"sabd{pg2}", ld)])
            _dump("rowspa0", rbuf[0:1, 0:1024], [1, 1024], F32,
                  [(f"sabc{pg2}", wcnt)])
        return (f"sabd{pg2}", ld, pg2, ci)

    def em_chain(i):
        par = i % 4
        pa2 = i % 2
        mu_o = 512 * par
        va_o = 512 * par + 256
        rpp = sb["rows_pp"]
        xg2 = sb["den"][:, :, :].rearrange("p a b -> p (a b)").rearrange(
            "p (c t) -> p c t", c=8)
        if i < 8:
            xa = [sb["a_xown"][:, c, 256 * i:256 * (i + 1)]
                  for c in range(8)]
            sp_x = ("sxo", M["xown"])
        else:
            xgt = [sb["xg"], xg2, sb["xg3"], sb["xg4"]][par]
            sp_x = (f"sx{par}", M[("xload", i)])
            xa = [xgt[:, c, :] for c in range(8)]
        def f_sx(t, xa=xa, pa2=pa2):
            r = None
            for c in range(8):
                r = t.matmul(psA[0:1, 512 * pa2:512 * pa2 + 256],
                             sb["onescol"][:, :], xa[c],
                             start=(c == 0), stop=(c == 7))
            return r
        pe_sx = P.emit("pe", f_sx,
                       waits=[sp_x, ("dve", M["memsets"]),
                              ("dve", M.get(("pp_mu", i - 2), 0))])
        M[("pp_sxe", i)] = pe_sx
        last_pe = None
        d_sq = None
        for c in range(8):
            pg = c
            wd = [("pe", M.get(("pp_sq", i - 1, c), 0))]
            if c == 0:
                wd.append(sp_x)
            def f_sq(v, xac=xa[c], pg=pg):
                return v.tensor_mul(sb["sq"][:, pg, :], xac, xac)
            d_sq = P.emit("dve", f_sq, waits=wd)
            w_m2 = [("dve", d_sq)]
            if c == 0:
                w_m2.append(("dve", M.get(("pp_var", i - 2), 0)))
            def f_sx2(t, c=c, pg=pg, pa2=pa2):
                return t.matmul(psB[0:1, 512 * pa2:512 * pa2 + 256],
                                sb["onescol"][:, :], sb["sq"][:, pg, :],
                                start=(c == 0), stop=(c == 7))
            last_pe = P.emit("pe", f_sx2, waits=w_m2)
            M[("pp_sq", i, c)] = last_pe
        M[("pp_sx2", i)] = last_pe
        M[("pp_dsq", i)] = d_sq
        def f_mu(v, mu_o=mu_o, pa2=pa2):
            return v.tensor_scalar(rpp[0:1, mu_o:mu_o + 256],
                                   psA[0:1, 512 * pa2:512 * pa2 + 256],
                                   cp("inv1024", 0, 1), None, ALU.mult)
        M[("pp_mu", i)] = P.emit(
            "dve", f_mu,
            waits=[("pe", pe_sx),
                   (f"sabw{par}", M.get(("pp_ab", i - 4), 0))])
        def f_mu2(v, mu_o=mu_o, va_o=va_o):
            return v.tensor_mul(rpp[0:1, va_o:va_o + 256],
                                rpp[0:1, mu_o:mu_o + 256],
                                rpp[0:1, mu_o:mu_o + 256])
        P.emit("dve", f_mu2)
        def f_var(v, va_o=va_o, pa2=pa2):
            return v.scalar_tensor_tensor(
                rpp[0:1, va_o:va_o + 256],
                psB[0:1, 512 * pa2:512 * pa2 + 256],
                cp("inv1024", 0, 1), rpp[0:1, va_o:va_o + 256],
                ALU.mult, ALU.subtract)
        M[("pp_var", i)] = P.emit("dve", f_var, waits=[("pe", last_pe)])
        def f_ln(s, va_o=va_o):
            return s.activation(rpp[0:1, va_o:va_o + 256],
                                rpp[0:1, va_o:va_o + 256], AF.Ln,
                                bias=cp("eps", 0, 1))
        P.emit("act", f_ln, waits=[("dve", M[("pp_var", i)])])
        def f_exp(s, va_o=va_o):
            return s.activation(rpp[0:1, va_o:va_o + 256],
                                rpp[0:1, va_o:va_o + 256], AF.Exp,
                                scale=cp("mhalf", 0, 1))
        a_a = P.emit("act", f_exp)
        def f_b(v, mu_o=mu_o, va_o=va_o):
            return v.scalar_tensor_tensor(
                rpp[0:1, mu_o:mu_o + 256], rpp[0:1, mu_o:mu_o + 256],
                cp("mone", 0, 1), rpp[0:1, va_o:va_o + 256],
                ALU.mult, ALU.mult)
        M[("pp_b", i)] = P.emit("dve", f_b, waits=[("act", a_a)])
        def f_ab(e, i=i, par=par):
            return e.dma_start(dr["ab"][i:i + 1, :],
                               rpp[0:1, 512 * par:512 * par + 512])
        M[("pp_ab", i)] = P.emit("sp", f_ab, stream=f"sabw{par}",
                                 waits=[("dve", M[("pp_b", i)])])

    def em_xload(i):
        par = i % 4
        xg2 = sb["den"][:, :, :].rearrange("p a b -> p (a b)").rearrange(
            "p (c t) -> p c t", c=8)
        xgt = [sb["xg"], xg2, sb["xg3"], sb["xg4"]][par]
        w_ld = [("pe", M.get(("pp_sxe", i - 4), 0)),
                ("dve", M.get(("pp_dsq", i - 4), 0))]
        def f_ld(e, s=dr["x_tiled"][i - 8], xgt=xgt):
            return e.dma_start(xgt[:, :, :], s)
        M[("xload", i)] = P.emit("sp", f_ld, waits=w_ld, stream=f"sx{par}")

    def em_qab(idx):
        w = [(f"sabw{idx % 4}", M[("pp_ab", idx)]),
             ("gpa", M.get(("q_apply", idx - 2), 0))]
        def f_ab(e, idx=idx):
            return e.dma_start(
                sb["absq"][:, idx % 2, :],
                dr["ab"].ap().partition_broadcast(128)[:, idx, :])
        strm = "sab0" if idx % 2 == 0 else "sab1"
        M[("qab", idx)] = P.emit("sp", f_ab, waits=w, stream=strm)

    def em_qapply(t, j):
        idx = 2 * t + j
        strm = "sab0" if idx % 2 == 0 else "sab1"
        tq = sb[f"tlq{t}"]
        d_apply = None
        for c in range(8):
            xac = sb["a_xown"][:, c, 256 * idx:256 * (idx + 1)]
            wt = [(strm, M[("qab", idx)]), ("sxo", M["xown"])]
            if c == 0 and j == 0:
                wt.append(("pe", M.get(("qproj_pe", t - 1), 0)))
            def f_t(v, xac=xac, tq=tq, idx=idx):
                return v.tensor_mul(tq[:, :], xac,
                                    sb["absq"][:, idx % 2, 256:512])
            P.emit("gp", f_t, waits=wt, stream="gpa", inc=1)
            def f_x(v, c=c, j=j, tq=tq, idx=idx):
                return v.tensor_add(
                    sb["xln"][:, c, 256 * j:256 * (j + 1)],
                    tq[:, :], sb["absq"][:, idx % 2, 0:256])
            d_apply = P.emit("gp", f_x, stream="gpa", inc=1)
        M[("q_apply", idx)] = d_apply

    def em_qproj(t):
        d_apply = M[("q_apply", 2 * t + 1)]
        ga = [("gpa", d_apply)]
        if t == 0:
            _dump("xlnt0", sb["xln"][:, :, :], [128, 8, 512], BF16,
                  [("gpa", d_apply)])
        for cc in range(8):
            pg = cc % 2
            fl = 8 * t + cc
            def f_mm(t_, cc=cc, pg=pg):
                r = None
                for dc in range(8):
                    r = t_.matmul(bank(psA, 2 + pg),
                                  sb["wq_sb"][:, dc,
                                              128 * cc:128 * cc + 128],
                                  sb["xln"][:, dc, :],
                                  start=(dc == 0), stop=(dc == 7))
                return r
            w = ga + [("gq", M["wq"]),
                 ("act", M.get(("qt_tanh", fl - 2), 0)), ("sv", M["vecs"])]
            if _DEBUG_DUMPS:
                w += [("sdump", P.cnt["sdump"])]
            pe_q = P.emit("pe", f_mm, waits=w)
            M[("qproj_pe", t)] = pe_q
            def f_tanh(s, t=t, cc=cc, pg=pg):
                return s.activation(
                    sb["b_q"][:, cc, 512 * t:512 * (t + 1)],
                    bank(psA, 2 + pg), AF.Tanh, bias=sb["bq"][:, cc:cc + 1])
            M[("qt_tanh", fl)] = P.emit("act", f_tanh, waits=[("pe", pe_q)])

    def em_qfinish():
        M["q_done_act"] = P.cnt["act"]
        M["q_done_pe"] = P.cnt["pe"]

        def f_vst(v):
            v.memset(sb["vst"][:, :, :, :], 0.0)
            r = None
            for h in range(16):
                j = h if (h % 2 == 1) else 64 + h
                r = v.memset(sb["vst"][:, :, h, j:j + 1], 1.0)
            return r
        M["vst_init"] = P.emit(
            "dve", f_vst,
            waits=[("pe", M["q_done_pe"]), ("act", M["q_done_act"]),
                   ("pe", M[("pp_sx2", 39)]),
                   ("sabw0", M[("pp_ab", 36)]),
                   ("sabw1", M[("pp_ab", 37)]),
                   ("sabw2", M[("pp_ab", 38)]),
                   ("sabw3", M[("pp_ab", 39)])])

    vstate = {"v": 0}
    E_hist, EA_hist, D_hist = {}, {}, {}

    def em_group_ln(g):
        sp_xs = []
        for j in range(2):
            H = 2 * g + j
            if g == 0:
                w_load = [("dve", M["memsets"]),
                          ("pe", M.get(("pp_sx2", 39), 0)),
                          ("dve", M[("pp_b", 39)])]
            else:
                w_load = [("gga", M[(g - 1, j, "apply_last")])]
            buf = [sb["xg"], sb["gxb"]][H % 2]
            def f_ld(e, s=dr["x_tiled"][H], buf=buf):
                return e.dma_start(buf[:, :, :], s)
            sp_xs.append(P.emit("sp", f_ld, waits=w_load,
                                stream=f"sgx{H % 2}"))
        d_apply = None
        for j in range(2):
            idx = 8 + 2 * g + j
            H = 2 * g + j
            buf = [sb["xg"], sb["gxb"]][H % 2]
            wab = [(f"sabw{idx % 4}", M[("pp_ab", idx)]),
                   ("gga", M.get((g - 1, j, "apply_last"), 0))]
            def f_ab(e, idx=idx, j=j):
                return e.dma_start(
                    sb["absg"][:, j, :],
                    dr["ab"].ap().partition_broadcast(128)[:, idx, :])
            strm = "sag0" if j == 0 else "sag1"
            sp_ab = P.emit("sp", f_ab, waits=wab, stream=strm)
            for c in range(8):
                wt = [(strm, sp_ab), (f"sgx{H % 2}", sp_xs[j])]
                if c == 0 and j == 0:
                    wt.append(("pe", M.get((g - 1, "kv_last"),
                                           M.get(("pp_sx2", 39), 0))))
                def f_t(v, c=c, buf=buf, j=j):
                    return v.tensor_mul(sb["tl"][:, 0:256], buf[:, c, :],
                                        sb["absg"][:, j, 256:512])
                P.emit("gp", f_t, waits=wt, stream="gga", inc=1)
                def f_x(v, c=c, j=j):
                    return v.tensor_add(
                        sb["xln"][:, c, 256 * j:256 * (j + 1)],
                        sb["tl"][:, 0:256], sb["absg"][:, j, 0:256])
                d_apply = P.emit("gp", f_x, stream="gga", inc=1)
            M[(g, j, "apply_last")] = d_apply
        M[(g, "apply_last")] = d_apply

    def em_group_kv(g):
        apply_done = M[(g, "apply_last")]
        kv_first_act = P.cnt["act"]
        for cc in range(8):
            pgw = cc % 2
            fl = 8 * g + cc
            def f_wk(e, cc=cc, pgw=pgw):
                return e.dma_start(sb["wkb"][:, pgw, :, :], dr["wk_bf"][cc])
            sp_w = P.emit("sp", f_wk, stream=f"swk{fl % 2}",
                          waits=[("pe", M.get(("kmmf", fl - 2), 0)),
                                 ("gwk", M["wk_bf"])])
            pg = 2 + (cc % 2)
            def f_mm(t_, cc=cc, pg=pg, pgw=pgw):
                r = None
                for dc in range(8):
                    r = t_.matmul(bank(psB, pg), sb["wkb"][:, pgw, dc, :],
                                  sb["xln"][:, dc, :],
                                  start=(dc == 0), stop=(dc == 7))
                return r
            pe_k = P.emit("pe", f_mm,
                          waits=[(f"swk{fl % 2}", sp_w),
                                 ("gga", apply_done),
                                 ("act", M.get((g, "ktanh", cc - 1), 0))])
            M[("kmmf", fl)] = pe_k
            def f_tanh(s, cc=cc, pg=pg):
                return s.activation(sb["kg"][:, cc, :], bank(psB, pg),
                                    AF.Tanh, bias=sb["bk"][:, cc:cc + 1])
            M[(g, "ktanh", cc)] = P.emit("act", f_tanh, waits=[("pe", pe_k)])

        vc_last = 0
        for kc in range(4):
            for cb in range(2):
                pg = (2 * kc + cb) % 2
                def f_mm(t_, kc=kc, cb=cb, pg=pg):
                    r = t_.matmul(bank(psB, pg), sb["onesrow_bf"][:, :],
                                  sb["bv_bf"][0:1, 512 * cb:512 * (cb + 1)],
                                  start=True, stop=False)
                    for dc in range(8):
                        r = t_.matmul(
                            bank(psB, pg),
                            sb["xln"][:, dc, 128 * kc:128 * (kc + 1)],
                            sb["wv_sb"][:, dc, 512 * cb:512 * (cb + 1)],
                            start=False, stop=(dc == 7))
                    return r
                w_mm = [("gwv", M["wv"]), ("gbv", M["bv"]),
                        ("gga", apply_done),
                        ("act", M.get((g, "ktanh", 7), 0)),
                        ("act", M.get((g, "vtanh", 2 * kc + cb - 1), 0))]
                pe_v = P.emit("pe", f_mm, waits=w_mm)
                def f_tanh(s, cb=cb, pg=pg):
                    return s.activation(sb["vstg"][:, cb, :], bank(psB, pg),
                                        AF.Tanh)
                w_t = [("pe", pe_v),
                       ("dve", M.get((g, "vcopy", kc - 1, cb), 0))]
                a_v = P.emit("act", f_tanh, waits=w_t)
                M[(g, "vtanh", 2 * kc + cb)] = a_v
                def f_cp(v, kc=kc, cb=cb):
                    r = None
                    for hh in range(8):
                        h = cb * 8 + hh
                        r = v.tensor_copy(
                            sb["vst"][:, kc, h,
                                      64 * (h % 2):64 * (h % 2) + 64],
                            sb["vstg"][:, cb, 64 * hh:64 * hh + 64])
                    return r
                vc_last = P.emit("dve", f_cp, waits=[("act", a_v)])
                M[(g, "vcopy", kc, cb)] = vc_last
        M[(g, "kv_last")] = P.cnt["pe"]
        M[(g, "vcopy_last")] = vc_last
        M[(g, "kv_act_last")] = P.cnt["act"]

    def em_avd(g, v, h, k, psX, pbt):
        def f_AV(t_, h=h, psX=psX, pbt=pbt):
            r = None
            for kc in range(4):
                r = t_.matmul(bank(psX, 0), sb["vst"][:, kc, h, :],
                              pbt[:, kc, :],
                              start=(kc == 0), stop=(kc == 3))
            return r
        wA = [("act", E_hist[v])]
        if v == M.get(("gstart", g), -1):
            wA.append(("dve", M[(g, "vcopy_last")]))
        A_v = P.emit("pe", f_AV, waits=wA)
        first_ctx = (g == 0)
        first_den = (g == 0 and h < 2)
        def f_D(vv, h=h, k=k, psX=psX, first_ctx=first_ctx,
                first_den=first_den):
            pb = 64 * (h % 2)
            cc = h // 2
            ctx_dst = sb["a_ctx"][pb:pb + 64, cc, 512 * k:512 * (k + 1)]
            den_dst = sb["den"][64 - pb:128 - pb, k, :]
            ctx_src = psX[pb:pb + 64, 0:512]
            den_src = psX[64 - pb:128 - pb, 0:512]
            if first_ctx:
                vv.tensor_copy(ctx_dst, ctx_src)
            else:
                vv.tensor_add(ctx_dst, ctx_dst, ctx_src)
            if first_den:
                return vv.tensor_copy(den_dst, den_src)
            return vv.tensor_add(den_dst, den_dst, den_src)
        D_hist[v] = P.emit("dve", f_D, waits=[("pe", A_v)])

    def em_visits(g):
        k0 = g // 4
        slots = [k for k in range(k0 + 1, 4)] + [k0]
        M[("gstart", g)] = vstate["v"]
        pend = None
        nvis = 0
        inject_at = 16 * len(slots) // 2
        for k in slots:
            diag = (k == k0)
            for h in range(16):
                if nvis == inject_at and g + 1 < NG:
                    em_group_ln(g + 1)
                nvis += 1
                v = vstate["v"]
                psX = ps[v % 2]
                pbt = [sb["pbuf"], sb["pbuf2"]][v % 2]
                wS = [("act", EA_hist.get(v - 2, 0))]
                if v % 2 == 0:
                    wS.append(("act", M[(g, "ktanh", h // 2)]))
                else:
                    wS.append(("act", M[(g, "kv_act_last")]))
                if diag:
                    wS.append(("srs", M[(g, "rsel")]))
                rsb = sb["rselb"] if g % 2 == 0 else sb["rselb2"]
                def f_Sk(t_, k=k, h=h, diag=diag, psX=psX, rsb=rsb,
                         kcs=(1, 2, 3)):
                    cc, pb = h // 2, 64 * (h % 2)
                    r = None
                    for kc in kcs:
                        r = t_.matmul(
                            bank(psX, kc),
                            sb["kg"][pb:pb + 64, cc, 128 * kc:128 * (kc + 1)],
                            sb["b_q"][pb:pb + 64, cc, 512 * k:512 * (k + 1)],
                            start=True, stop=(not diag))
                        if diag:
                            r = t_.matmul(bank(psX, kc), sb["lmat"][:, :],
                                          rsb[:, kc, :],
                                          start=False, stop=True)
                    return r
                Sa_v = P.emit("pe", f_Sk, waits=wS)
                wEa = [("pe", Sa_v)]
                if v == 1:
                    wEa += [("pe", M[("pp_sx2", 38)]),
                            ("dve", M[("pp_dsq", 38)])]


                def f_Ea(s, psX=psX, pbt=pbt):
                    return s.activation(
                        pbt[:, 1:4, :],
                        psX[:, 512:2048].rearrange("p (a b) -> p a b", a=3),
                        AF.Exp, bias=cp("mexp"), scale=cp("scale8"))
                EA_hist[v] = P.emit("act", f_Ea, waits=wEa)
                if pend is not None:
                    em_avd(*pend)
                    pend = None
                def f_Sb(t_, k=k, h=h, diag=diag, psX=psX):
                    return f_Sk(t_, k=k, h=h, diag=diag, psX=psX, kcs=(0,))
                S_v = P.emit("pe", f_Sb,
                             waits=[("dve", D_hist.get(v - 2, 0))])
                def f_Eb(s, psX=psX, pbt=pbt):
                    return s.activation(pbt[:, 0, :], psX[:, 0:512],
                                        AF.Exp, bias=cp("mexp"),
                                        scale=cp("scale8"))
                E_hist[v] = P.emit("act", f_Eb, waits=[("pe", S_v)])
                pend = (g, v, h, k, psX, pbt)
                vstate["v"] += 1
        if _PIPE_AV:
            em_avd(*pend)
        if g + 1 < NG:
            dst = sb["rselb2"] if (g + 1) % 2 == 1 else sb["rselb"]
            def f_rs(e, g=g, dst=dst):
                return e.dma_start(dst[:, :, :], dr["rsel"][g + 1])
            M[(g + 1, "rsel")] = P.emit(
                "sp", f_rs, stream="srs",
                waits=[("pe", M.get((g - 1, "vend_pe"), 0))])

    def em_postattn():
        last_v = vstate["v"] - 1
        drain = [("dve", D_hist[last_v]), ("act", E_hist[last_v])]
        def f_sel(e):
            return e.dma_start(sb["sel_sb"][:, :, :],
                               dr["SEL"].ap())
        sp_sel = P.emit("sp", f_sel, waits=drain, stream="sv")
        def f_wo(e):
            return e.dma_start(
                sb["wo_sb"][:, :, :],
                dr["wo"].ap())
        gp_wo = P.emit("gp", f_wo, waits=drain, stream="gwo")
        def f_ln(s):
            return s.activation(sb["lnd"][:, :, :], sb["den"][:, :, :],
                                AF.Ln, bias=cp("tiny"))
        w_ln = list(drain)
        if _DEBUG_DUMPS:
            w_ln += [("sdump", P.cnt["sdump"])]
        P.emit("act", f_ln, waits=w_ln)
        def f_ex(s):
            return s.activation(sb["den"][:, :, :], sb["lnd"][:, :, :],
                                AF.Exp, scale=cp("mone"))
        a2 = P.emit("act", f_ex)
        lastd = 0
        for c in range(8):
            for k in range(4):
                fl = 4 * c + k
                pg = fl % 2
                def f_mm(t_, c=c, k=k, pg=pg):
                    return t_.matmul(bank(psA, pg), sb["sel_sb"][:, c, :],
                                     sb["den"][:, k, :], start=True,
                                     stop=True)
                pe_m = P.emit("pe", f_mm,
                              waits=[("act", a2), ("sv", sp_sel),
                                     ("dve", M.get(("ctxn", fl - 2), 0))])
                def f_mul(vv, c=c, k=k, pg=pg):
                    return vv.tensor_mul(
                        sb["b_ctxfm"][:, c, 512 * k:512 * (k + 1)],
                        sb["a_ctx"][:, c, 512 * k:512 * (k + 1)],
                        bank(psA, pg))
                lastd = P.emit("dve", f_mul, waits=[("pe", pe_m)])
                M[("ctxn", fl)] = lastd
        M["ctxn_last"] = lastd
        _dump("rdn", sb["den"][:, :, :], [128, 4, 512], F32, [("act", a2)])
        _dump("ctxfm", sb["b_ctxfm"][:, :, :], [128, 8, TOK], BF16,
              [("dve", lastd)])

        for tb in range(4):
            def f_xo(e, tb=tb):
                return e.dma_start(sb["p_xot"][:, :, :], dr["x_own_tb"][tb])
            sp_xo = P.emit("sp", f_xo, stream="sxo",
                           waits=[("dve", M.get(("pa_xo", tb - 1), 0))])
            for oc in range(8):
                def f_mm(t_, tb=tb, oc=oc):
                    r = None
                    for ccc in range(8):
                        r = t_.matmul(
                            bank(psA, oc % 4),
                            sb["wo_sb"][:, ccc, 128 * oc:128 * oc + 128],
                            sb["b_ctxfm"][:, ccc, 512 * tb:512 * (tb + 1)],
                            start=(ccc == 0), stop=(ccc == 7))
                    return r
                w = [("gwo", gp_wo), ("dve", M["ctxn_last"])]
                if oc >= 4:
                    w.append(("act", M.get(("pa_tanh", tb, oc - 4), 0)))
                else:
                    w.append(("act", M.get(("pa_tanh", tb - 1, oc + 4), 0)))
                if _DEBUG_DUMPS:
                    w += [("sdump", P.cnt["sdump"])]
                pe_o = P.emit("pe", f_mm, waits=w)
                wt = [("pe", pe_o),
                      ("dve", M.get(("pa_ft", tb - 1, oc), 0)),
                      ("pe", M.get((("pa", tb - 1), "sx_pe"), 0))]
                def f_tanh(s, oc=oc):
                    return s.activation(sb["p_y"][:, oc, :],
                                        bank(psA, oc % 4), AF.Tanh,
                                        bias=sb["bo"][:, oc:oc + 1])
                M[("pa_tanh", tb, oc)] = P.emit("act", f_tanh, waits=wt)
            if tb == 0:
                _dump("y0", sb["p_y"][:, :, :], [128, 8, 512], F32,
                      [("act", M[("pa_tanh", 0, 7)])])
            ya = [sb["p_y"][:, c, :] for c in range(8)]
            tw = [("act", M[("pa_tanh", tb, 7)])]
            sd, ld, apg, ci = em_stats_chain(ya, 512, ("pa", tb), sb["p_sq"],
                                             extra_pe=tw, extra_dve=tw)
            d_last = None
            for c in range(8):
                def f_t(vv, c=c, apg=apg):
                    return vv.tensor_mul(sb["p_sq2"][:, c % 2, :],
                                         sb["p_y"][:, c, :],
                                         sb["absc"][:, apg, 512:1024])
                M[("pa_ft", tb, c)] = P.emit("dve", f_t, waits=[(sd, ld)])
                def f_t2(vv, c=c, apg=apg):
                    return vv.tensor_add(sb["p_t2"][:, c, :],
                                         sb["p_sq2"][:, c % 2, :],
                                         sb["absc"][:, apg, 0:512])
                P.emit("dve", f_t2)
                def f_x1(vv, c=c, tb=tb):
                    return vv.tensor_add(
                        sb["a_x1"][:, c, 512 * tb:512 * (tb + 1)],
                        sb["p_xot"][:, c, :], sb["p_t2"][:, c, :])
                d_last = P.emit("dve", f_x1, waits=[("sxo", sp_xo)])
            M[("pa_xo", tb)] = d_last
            M[("abs_use", ci)] = d_last
            def f_sp_t2(e, tb=tb):
                return e.dma_start(dr["t2sp"][tb], sb["p_t2"][:, :, :])
            M[("t2spill", tb)] = P.emit("gp", f_sp_t2, stream="st2s",
                                        waits=[("dve", d_last)])
            x1a = [sb["a_x1"][:, c, 512 * tb:512 * (tb + 1)]
                   for c in range(8)]
            tw3 = [("dve", M[("pa_xo", tb)])]
            sd3, ld3, apg3, ci3 = em_stats_chain(x1a, 512, ("l3", tb),
                                                 sb["p_sq"], extra_pe=tw3,
                                                 extra_dve=tw3)
            d3 = None
            for c in range(8):
                def f_t3(vv, c=c, x1a=x1a, apg3=apg3):
                    return vv.tensor_mul(sb["p_sq2"][:, c % 2, :], x1a[c],
                                         sb["absc"][:, apg3, 512:1024])
                P.emit("dve", f_t3, waits=[(sd3, ld3)])
                def f_x3(vv, c=c, tb=tb, apg3=apg3):
                    return vv.tensor_add(
                        sb["b_xln2"][:, c, 512 * tb:512 * (tb + 1)],
                        sb["p_sq2"][:, c % 2, :],
                        sb["absc"][:, apg3, 0:512])
                d3 = P.emit("dve", f_x3)
            M[("abs_use", ci3)] = d3
        M["xln2_done"] = P.cnt["dve"]

    def em_ffn():
        for hc in range(32):
            pgw = hc % 2
            def f_w1(e, hc=hc, pgw=pgw):
                return e.dma_start(sb["wst"][:, pgw, 0:16, :],
                                   dr["w12t_bf"][hc])
            w_ld = [("pe", M.get(("s1mm", 4 * (hc - 2) + 3), 0)),
                    ("gw1", M["w12t_bf"])]
            if hc < 2:
                w_ld += [("act", M["q_done_act"]),
                         ("dve", M["xln2_done"]),
                         ("gga", M[(NG - 1, "apply_last")])]
            sp2 = P.emit("sp", f_w1, waits=w_ld, stream=f"swst{hc % 2}")
            for tb in range(4):
                fl = 4 * hc + tb
                r0 = 2 * (fl % 4)
                pst1 = psA if r0 < 4 else psB
                pst2 = psA if (r0 + 1) < 4 else psB
                b1, b2 = r0 % 4, (r0 + 1) % 4
                def f_mm(t_, tb=tb, pgw=pgw, pst1=pst1, pst2=pst2,
                         b1=b1, b2=b2):
                    r = None
                    for dc in range(8):
                        r = t_.matmul(bank(pst1, b1), sb["wst"][:, pgw, dc, :],
                                      sb["b_xln2"][:, dc,
                                                   512 * tb:512 * (tb + 1)],
                                      start=(dc == 0), stop=(dc == 7))
                    for dc in range(8):
                        r = t_.matmul(bank(pst2, b2),
                                      sb["wst"][:, pgw, 8 + dc, :],
                                      sb["b_xln2"][:, dc,
                                                   512 * tb:512 * (tb + 1)],
                                      start=(dc == 0), stop=(dc == 7))
                    return r
                w_mm = [(f"swst{hc % 2}", sp2), ("dve", M["xln2_done"]),
                        ("dve", M.get(("s1dve", fl - 4), 0)),
                        ("act", M.get(("s1act", fl - 4), 0))]
                if fl == 0:
                    w_mm += [("st2s", M[("t2spill", 3)])]
                    if _DEBUG_DUMPS:
                        w_mm += [("sdump", P.cnt["sdump"])]
                pe_1 = P.emit("pe", f_mm, waits=w_mm)
                M[("s1mm", fl)] = pe_1
                pga = tb % 2
                def f_silu(s, hc=hc, pga=pga, pst1=pst1, b1=b1):
                    return s.activation(sb["c_g1"][:, pga, :], bank(pst1, b1),
                                        AF.Silu, scale=sb["s1"][:, hc:hc + 1])
                a_s = P.emit("act", f_silu,
                             waits=[("pe", pe_1),
                                    ("dve", M.get(("s1dve", fl - 2), 0))])
                M[("s1act", fl)] = a_s
                hv = sb["a_ha"] if hc < 16 else sb["c_hb"]
                def f_h(vv, hc=hc, tb=tb, pga=pga, hv=hv, pst2=pst2, b2=b2):
                    return vv.scalar_tensor_tensor(
                        hv[:, hc % 16, 512 * tb:512 * (tb + 1)],
                        bank(pst2, b2), sb["s2"][:, hc:hc + 1],
                        sb["c_g1"][:, pga, :], ALU.mult, ALU.mult)
                M[("s1dve", fl)] = P.emit("dve", f_h, waits=[("act", a_s)])
        M["s1_done_pe"] = P.cnt["pe"]
        M["s1_done_dve"] = P.cnt["dve"]
        _dump("ha", sb["a_ha"][:, :, :], [128, 16, TOK], BF16,
              [("dve", M["s1_done_dve"])])
        _dump("hb", sb["c_hb"][:, :, :], [128, 16, TOK], BF16,
              [("dve", M["s1_done_dve"])])

        def em_hstage(tag, wsrc, wgate, scale_t, outdram, first_waits,
                      split_tb=False):
            for hc2 in range(32):
                pgw = hc2 % 2
                def f_w(e, hc2=hc2, pgw=pgw, wsrc=wsrc):
                    return e.dma_start(sb["wst"][:, pgw, :, :], wsrc[hc2])
                w_ld = [("pe", M.get((tag, "mm", 4 * (hc2 - 2) + 3), 0)),
                        wgate]
                if hc2 < 2:
                    w_ld += [("pe", M[(tag, "prev_pe")])]
                sp_w = P.emit("sp", f_w, waits=w_ld,
                              stream=f"swst{hc2 % 2}")
                for tb in range(4):
                    fl = 4 * hc2 + tb
                    bk_ = fl % 8
                    pst = psA if bk_ < 4 else psB
                    def f_mm(t_, tb=tb, pgw=pgw, pst=pst, bk_=bk_):
                        r = None
                        for dc in range(32):
                            hv = sb["a_ha"] if dc < 16 else sb["c_hb"]
                            r = t_.matmul(bank(pst, bk_ % 4),
                                          sb["wst"][:, pgw, dc, :],
                                          hv[:, dc % 16,
                                             512 * tb:512 * (tb + 1)],
                                          start=(dc == 0), stop=(dc == 31))
                        return r
                    w_mm = [(f"swst{hc2 % 2}", sp_w),
                            ("act", M.get((tag, "act", fl - 8), 0))]
                    if fl == 0:
                        w_mm += first_waits
                    pe_m = P.emit("pe", f_mm, waits=w_mm)
                    M[(tag, "mm", fl)] = pe_m
                    pga = hc2 % 2
                    def f_gelu(s, hc2=hc2, tb=tb, pga=pga, pst=pst, bk_=bk_):
                        return s.activation(
                            sb["c_hp"][:, pga, 512 * tb:512 * (tb + 1)],
                            bank(pst, bk_ % 4), AF.Gelu,
                            scale=scale_t[:, hc2:hc2 + 1])
                    a_g = P.emit("act", f_gelu,
                                 waits=[("pe", pe_m),
                                        (f"sspg{hc2 % 2}",
                                         M.get((tag, "spill", hc2 - 2), 0))])
                    M[(tag, "act", fl)] = a_g
                if split_tb:
                    last = 0
                    for tb2 in range(4):
                        def f_out(e, hc2=hc2, pga=pga, tb2=tb2):
                            return e.dma_start(
                                outdram[tb2, :, hc2, :],
                                sb["c_hp"][:, pga, 512 * tb2:512 * (tb2 + 1)])
                        last = P.emit("gp", f_out, stream=f"sspg{hc2 % 2}",
                                      waits=[("act", a_g)])
                    M[(tag, "spill", hc2)] = last
                else:
                    def f_out(e, hc2=hc2, pga=pga):
                        return e.dma_start(outdram[hc2, :, :],
                                           sb["c_hp"][:, pga, :])
                    M[(tag, "spill", hc2)] = P.emit(
                        "gp", f_out, stream=f"sspg{hc2 % 2}",
                        waits=[("act", a_g)])

        M[("h1", "prev_pe")] = M["s1_done_pe"]
        em_hstage("h1", dr["h1t_bf"], ("gh1", M["h1t_bf"]), sb["sh1"],
                  dr["hp"], [("dve", M["s1_done_dve"])])
        _dump("hp", dr["hp"].ap(), [32, 128, TOK], BF16,
              [("sspg0", M[("h1", "spill", 30)]),
               ("sspg1", M[("h1", "spill", 31)])])

        for hc in range(32):
            hv = sb["a_ha"] if hc < 16 else sb["c_hb"]
            def f_ld(e, hc=hc, hv=hv):
                return e.dma_start(hv[:, hc % 16, :], dr["hp"][hc])
            eng_h = "sp" if hc % 2 == 0 else "gp"
            M[("hpload", hc)] = P.emit(
                eng_h, f_ld, stream="shp" if hc % 2 == 0 else "shpg",
                waits=[("pe", M[("h1", "mm", 127)]),
                       (f"sspg{hc % 2}", M[("h1", "spill", hc)])])

        M[("h2", "prev_pe")] = M[("h1", "mm", 127)]
        em_hstage("h2", dr["h2t_bf"], ("gh2", M["h2t_bf"]), sb["sh2"],
                  dr["hpp"], [("shp", M[("hpload", 30)]),
                              ("shpg", M[("hpload", 31)])], split_tb=True)

        pyb = [sb["p_y"], sb["p_y2"]]
        for oc3 in range(4):
            def f_wr(e, oc3=oc3):
                return e.dma_start(sb["w3r"][:, oc3, :, :],
                                   dr["w3t_bf"][oc3])
            M["w3r_ld"] = P.emit(
                "sp", f_wr, stream="sw3r",
                waits=[("pe", M["s1_done_pe"]), ("gw3", M["w3t_bf"])])
        for pr in range(2):
            sp_hs = {}
            for t_ in range(2):
                tb = 2 * pr + t_
                def f_hpp(e, tb=tb, t_=t_):
                    return e.dma_start(sb["a_hppt"][:, t_, :, :],
                                       dr["hpp"][tb])
                w = [("sspg0", M[("h2", "spill", 30)]),
                     ("sspg1", M[("h2", "spill", 31)]),
                     ("pe", M.get(("w3pair_mm", pr - 1), 0))]
                sp_hs[t_] = P.emit("sp", f_hpp, waits=w,
                                   stream=f"shp{t_}")
            pe_m = 0
            for oc in range(8):
                pgw = oc % 2
                if oc < 4:
                    wslc = sb["w3r"][:, oc, :, :]
                    wwait = ("sw3r", M["w3r_ld"])
                else:
                    def f_w(e, oc=oc, pgw=pgw):
                        return e.dma_start(sb["wst"][:, pgw, :, :],
                                           dr["w3t_bf"][oc])
                    w_ld = [("pe",
                             M.get(("w3mmf", 16 * pr + 2 * (oc - 2) + 1),
                                   0)),
                            ("gw3", M["w3t_bf"])]
                    if pr == 0 and oc < 6:
                        w_ld += [("pe", M[("h2", "mm", 127)])]
                    sp_w = P.emit("sp", f_w, waits=w_ld,
                                  stream=f"swst{oc % 2}")
                    wslc = sb["wst"][:, pgw, :, :]
                    wwait = (f"swst{oc % 2}", sp_w)
                for t_ in range(2):
                    tb = 2 * pr + t_
                    flw = 16 * pr + 2 * oc + t_
                    bb_ = flw % 4
                    def f_mm(t2_, wslc=wslc, t_=t_, bb_=bb_):
                        r = None
                        for dc in range(32):
                            r = t2_.matmul(bank(psA, bb_),
                                           wslc[:, dc, :],
                                           sb["a_hppt"][:, t_, dc, :],
                                           start=(dc == 0), stop=(dc == 31))
                        return r
                    w_mm = [wwait, (f"shp{t_}", sp_hs[t_]),
                            ("dve", M.get(("w3evf", flw - 4), 0))]
                    pe_m = P.emit("pe", f_mm, waits=w_mm)
                    M[("w3mmf", flw)] = pe_m
                    def f_ev(vv, oc=oc, t_=t_, bb_=bb_):
                        return vv.tensor_scalar(pyb[t_][:, oc, :],
                                                bank(psA, bb_),
                                                sb["s3"][:, oc:oc + 1], None,
                                                ALU.mult)
                    M[("w3evf", flw)] = P.emit(
                        "dve", f_ev,
                        waits=[("pe", pe_m),
                               ("pe", M.get((("l4", tb - 2), "sx_pe"), 0)),
                               (f"sout{t_}", M.get(("outdma", tb - 2), 0))])
                    M[("w3ev", tb, oc)] = M[("w3evf", flw)]
            M[("w3pair_mm", pr)] = pe_m
            for t_ in range(2):
                tb = 2 * pr + t_
                def f_xo(e, tb=tb):
                    return e.dma_start(sb["p_xot"][:, :, :],
                                       dr["x_own_tb"][tb])
                w_r = [("dve", M.get(("w3res", tb - 1), 0))]
                sp_xo = P.emit("sp", f_xo, waits=w_r, stream="sxo")
                def f_t2l(e, tb=tb):
                    return e.dma_start(sb["p_t2"][:, :, :], dr["t2sp"][tb])
                sp_t2 = P.emit("sp", f_t2l, stream="sxo")
                ya = [pyb[t_][:, c, :] for c in range(8)]
                tw = [("dve", M[("w3ev", tb, 7)])]
                sd4, ld4, apg4, ci4 = em_stats_chain(
                    ya, 512, ("l4", tb), sb["p_sq"], extra_pe=tw,
                    extra_dve=tw, abuf=sb["absc2"])
                d_last = None
                for c in range(8):
                    def f_t(vv, c=c, t_=t_, apg4=apg4):
                        return vv.tensor_mul(sb["p_sq2"][:, c % 2, :],
                                             pyb[t_][:, c, :],
                                             sb["absc2"][:, apg4, 512:1024])
                    P.emit("dve", f_t, waits=[(sd4, ld4)])
                    def f_t4(vv, c=c, t_=t_, apg4=apg4):
                        return vv.tensor_add(pyb[t_][:, c, :],
                                             sb["p_sq2"][:, c % 2, :],
                                             sb["absc2"][:, apg4, 0:512])
                    P.emit("dve", f_t4)
                    def f_o1(vv, c=c):
                        return vv.tensor_add(sb["p_sq2"][:, c % 2, :],
                                             sb["p_xot"][:, c, :],
                                             sb["p_t2"][:, c, :])
                    P.emit("dve", f_o1, waits=[("sxo", sp_t2),
                                               ("sxo", sp_xo)])
                    def f_o2(vv, c=c, t_=t_):
                        return vv.tensor_add(pyb[t_][:, c, :],
                                             pyb[t_][:, c, :],
                                             sb["p_sq2"][:, c % 2, :])
                    d_last = P.emit("dve", f_o2)
                M[("w3res", tb)] = d_last
                M[("abs_use", ci4)] = d_last
                def f_out(e, tb=tb, t_=t_):
                    return e.dma_start(dr["out"][tb], pyb[t_][:, :, :])
                M[("outdma", tb)] = P.emit("gp", f_out, stream=f"sout{t_}",
                                           waits=[("dve", d_last)])
        P.wait_only("sp", [("sout0", M[("outdma", 2)]),
                           ("sout1", M[("outdma", 3)])])

    em_prologue()
    def _pre(i):
        if 8 <= i + 3 <= 39:
            em_xload(i + 3)
        em_chain(i)
    for _i in range(8):
        if _i == 6:
            em_qab(0)
            em_qab(1)
        _pre(_i)
    for _t in range(4):
        em_qapply(_t, 0)
        _pre(8 * _t + 8)
        em_qapply(_t, 1)
        _pre(8 * _t + 9)
        if _t < 3:
            em_qab(2 * _t + 2)
            em_qab(2 * _t + 3)
        _pre(8 * _t + 10)
        _pre(8 * _t + 11)
        em_qproj(_t)
        for _i in range(8 * _t + 12, 8 * _t + 16):
            _pre(_i)
    em_qfinish()
    _casts = mk_cast_list()
    _dump("q", sb["b_q"][:, :, :], [128, 8, TOK], BF16,
          [("act", M["q_done_act"])])
    for g in range(NG):
        if g == 0:
            em_group_ln(0)
        em_group_kv(g)
        if g == 0:
            _dump("xln0", sb["xln"][:, :, :], [128, 8, 512], BF16,
                  [("gga", M[(0, "apply_last")])])
            _dump("kg0", sb["kg"][:, :, :], [128, 8, 512], BF16,
                  [("act", M[(0, "kv_act_last")])])
            _dump("vst0", sb["vst"][:, :, :, :], [128, 4, 16, 128], BF16,
                  [("dve", M[(0, "vcopy_last")])])
        em_visits(g)
        M[(g, "vend_pe")] = P.cnt["pe"]
        em_casts(_casts, 9, [("pe", M[(g, "kv_last")])])
    _lastv = vstate["v"] - 1
    _dump("ctx", sb["a_ctx"][:, :, :], [128, 8, TOK], F32,
          [("dve", D_hist[_lastv]), ("act", E_hist[_lastv])])
    _dump("den", sb["den"][:, :, :], [128, 4, 512], F32,
          [("dve", D_hist[_lastv]), ("act", E_hist[_lastv])])
    em_postattn()
    _dump("x1", sb["a_x1"][:, :, :], [128, 8, TOK], F32,
          [("dve", M[("pa_xo", 3)])])
    _dump("xln2", sb["b_xln2"][:, :, :], [128, 8, TOK], BF16,
          [("dve", M["xln2_done"])])
    _dump("t2", dr["t2sp"].ap(), [4, 128, 8, 512], F32,
          [("st2s", M[("t2spill", 3)])])
    _dump("ab", dr["ab"].ap(), [40, 512], F32,
          [("sabw0", M[("pp_ab", 36)]), ("sabw1", M[("pp_ab", 37)]),
           ("sabw2", M[("pp_ab", 38)]), ("sabw3", M[("pp_ab", 39)])])
    em_ffn()

    with contextlib.ExitStack() as stack:
        sem_objs = {s: stack.enter_context(nc.semaphore(f"sem_{s}"))
                    for s in Prog.STREAMS if P.cnt[s] > 0 or s in Prog.ENG}
        blk = stack.enter_context(nc.Block())

        def replay(ename):
            def run(eng):
                waited = {s: 0 for s in Prog.STREAMS}
                for waits, fn, inc, strm in P.ops[ename]:
                    for (s2, v) in waits:
                        if waited[s2] < v:
                            eng.wait_ge(sem_objs[s2], v)
                            waited[s2] = v
                    if fn is None:
                        continue
                    inst = fn(eng)
                    if inc:
                        inst.then_inc(sem_objs[strm], inc)
            return run

        blk.tensor(replay("pe"))
        blk.scalar(replay("act"))
        blk.vector(replay("dve"))
        blk.sync(replay("sp"))
        blk.gpsimd(replay("gp"))

    return nc


_NC_CACHE = [None]


def kernel(**inputs):
    per_core = build_host_inputs(inputs)
    if _NC_CACHE[0] is None:
        _NC_CACHE[0] = build_nc()
    nc = _NC_CACHE[0]
    res = run_bass_kernel_spmd(nc, per_core, core_ids=list(range(NCORE)))
    x = np.asarray(inputs["L0_attn_states"], np.float32)
    out = np.empty_like(x)
    for c in range(NCORE):
        b, i = c // 4, c % 4
        oc = np.asarray(res.results[c]["out"])
        for k in range(4):
            s = 4 * k + i
            out[b, 512 * s:512 * (s + 1), :] = (
                oc[k].transpose(2, 1, 0).reshape(512, 1024))
    return out
